# version 14
# baseline (speedup 1.0000x reference)
import numpy as np
import concourse.bass as bass
import concourse.mybir as mybir
from concourse.bass_utils import run_bass_kernel_spmd
from concourse.aot_env import allow_ici_all_to_all
from contextlib import ExitStack

F32 = mybir.dt.float32
BF16 = mybir.dt.bfloat16
AF = mybir.ActivationFunctionType
ALU = mybir.AluOpType
AX = mybir.AxisListType

NCORE = 8
D = 4096
S = 8192
TC = S // NCORE
NH = 4
DH = 128
FF = 11008
FC = 86
TBN, TBW = 32, 256
QTN, QTW = 16, 512
KCN = 64
JBN, JBW = 16, 256
DBN = 8
DBP = 4
THN = 2
SCALE = 0.08838834764831845

ENGS = ("t", "sc", "v", "g", "sy")


class Sem:
    __slots__ = ("h", "n")

    def __init__(self, h):
        self.h = h
        self.n = 0


def build_program(collect=None):
    nc = bass.Bass("TRN2", target_bir_lowering=False, debug=False)

    xT_img = nc.dram_tensor("xT_img", [128, TBN * 32 * TBW], F32, kind="ExternalInput")
    wq_img = nc.dram_tensor("wq_img", [128, 32 * 512], F32, kind="ExternalInput")
    wk_img = nc.dram_tensor("wk_img", [128, 32 * 512], F32, kind="ExternalInput")
    wv_img = nc.dram_tensor("wv_img", [128, 32 * 512], F32, kind="ExternalInput")
    wo_img = nc.dram_tensor("wo_img", [128, JBN * 32 * JBW], F32, kind="ExternalInput")
    wg_img = nc.dram_tensor("wg_img", [128, FC * 32 * 128], F32, kind="ExternalInput")
    wu_img = nc.dram_tensor("wu_img", [128, FC * 32 * 128], F32, kind="ExternalInput")
    wd_img = nc.dram_tensor("wd_img", [128, DBN * FC * 512], F32, kind="ExternalInput")
    x_img = nc.dram_tensor("x_img", [128, JBN * 8 * JBW], F32, kind="ExternalInput")
    idf_in = nc.dram_tensor("idf_in", [128, 128], F32, kind="ExternalInput")
    out = nc.dram_tensor("out", [TC, D], F32, kind="ExternalOutput")

    kT_d = nc.dram_tensor("kT_d", [NH * DH, S], BF16)
    qT_d = nc.dram_tensor("qT_d", [NH * DH, S], BF16)
    v_d = nc.dram_tensor("v_d", [128, KCN * NH * DH], BF16)
    a2a_inA = nc.dram_tensor("a2a_inA", [D, TC // 2], BF16)
    a2a_inB = nc.dram_tensor("a2a_inB", [D, TC // 2], BF16)
    a2a_outA = nc.dram_tensor("a2a_outA", [D, TC // 2], BF16)
    a2a_outB = nc.dram_tensor("a2a_outB", [D, TC // 2], BF16)
    r1_d = nc.dram_tensor("r1_d", [TC, D], F32)
    y1_d = nc.dram_tensor("y1_d", [TC, D], F32)
    y1T_d = nc.dram_tensor("y1T_d", [D, TC], BF16)
    r2_d = nc.dram_tensor("r2_d", [TC, D], F32)
    wg_bf = nc.dram_tensor("wg_bf", [128, FC * 32 * 128], BF16)
    wu_bf = nc.dram_tensor("wu_bf", [128, FC * 32 * 128], BF16)
    wd_bf = nc.dram_tensor("wd_bf", [128, DBN * FC * 512], BF16)

    Q = {k: [] for k in ENGS}

    def em(q, fn):
        Q[q].append(fn)

    def W(q, sem, val):
        if val > 0:
            em(q, lambda e, _s=sem.h, _v=val: e.wait_ge(_s, _v))

    def barrier(conds):
        for q in ENGS:
            for sem, val in conds:
                W(q, sem, val)

    ctx = ExitStack()
    with ctx:
        POOL_BF = ctx.enter_context(nc.sbuf_tensor("pool_bf", [128, 78848], BF16))
        POOL_F = ctx.enter_context(nc.sbuf_tensor("pool_f", [128, 12544], F32))
        idf = ctx.enter_context(nc.sbuf_tensor("idf", [128, 128], F32))
        ones_col = ctx.enter_context(nc.sbuf_tensor("ones_col", [128, 1], BF16))
        ones_row = ctx.enter_context(nc.sbuf_tensor("ones_row", [1, 128], BF16))
        epsb = ctx.enter_context(nc.sbuf_tensor("epsb", [128, 1], F32))
        ktbl = ctx.enter_context(nc.sbuf_tensor("ktbl", [1, 32], F32))
        qtbl = ctx.enter_context(nc.sbuf_tensor("qtbl", [1, 32], F32))
        kscal = ctx.enter_context(nc.sbuf_tensor("kscal", [1, 2], F32))
        tmp16 = ctx.enter_context(nc.sbuf_tensor("tmp16", [1, 32], F32))
        tmp16b = ctx.enter_context(nc.sbuf_tensor("tmp16b", [1, 32], F32))
        btbl = ctx.enter_context(nc.sbuf_tensor("btbl", [1, 32], BF16))

        ps_a = ctx.enter_context(nc.psum_tensor("ps_a", [128, 1024], F32))
        ps_b = ctx.enter_context(nc.psum_tensor("ps_b", [128, 1024], F32))
        ps_o = [ctx.enter_context(nc.psum_tensor(f"ps_o{i}", [128, 512], F32))
                for i in range(4)]
        ps = [ps_a[:, 0:512], ps_a[:, 512:1024],
              ps_b[:, 0:512], ps_b[:, 512:1024],
              ps_o[0][:, :], ps_o[1][:, :], ps_o[2][:, :], ps_o[3][:, :]]
        ps_pair = [ps_a, ps_b]

        def mk(name):
            return Sem(ctx.enter_context(nc.semaphore(name)))

        def mkl(name, n):
            return [mk(f"{name}{i}") for i in range(n)]

        sInit = mk("sInit")
        sPC = mk("sPC")
        sA_x = mkl("sA_x", 2)
        sA_w0 = mk("sA_w0"); sA_w1 = mk("sA_w1"); sA_w2 = mk("sA_w2")
        sA_mmq = mk("sA_mmq"); sA_mmv = mk("sA_mmv")
        sA_ev = mk("sA_ev"); sA_vev = mk("sA_vev")
        sA_qst = mkl("sA_qst", 8); sA_vst = mkl("sA_vst", 2)
        sB_ms = mk("sB_ms"); kvs = mkl("sB_kv", 4)
        sVcp = mk("sB_vcp")
        sB_sq = mk("sB_sq"); sB_nmm = mk("sB_nmm"); sB_nred = mk("sB_nred")
        sB_ch = mk("sB_ch"); sB_chA = mk("sB_chA"); sB_bt = mk("sB_bt")
        sB_bmm = mk("sB_bmm"); sB_bias = mk("sB_bias")
        sB_smm = mk("sB_smm"); sB_exp = mk("sB_exp"); sB_pvc = mk("sB_pvc")
        sB_rl = mk("sB_rl"); sB_osc = mk("sB_osc")
        sB_tr = mk("sB_tr"); sB_trev = mk("sB_trev"); sB_ost = mk("sB_ost")
        sCC = mk("sCC"); sCCa = mk("sCCa")
        sC_ot = mk("sC_ot"); sC_wo = mkl("sC_wo", 3); sC_x = mkl("sC_x", 2)
        sC_mm = mk("sC_mm"); sC_ev = mk("sC_ev"); sC_st = mkl("sC_st", 4)
        sL_in = mkl("sL_in", 2); sL_bn = mk("sL_bn"); sL_ag = mk("sL_ag")
        sL_sd = mk("sL_sd"); sL_rc = mk("sL_rc"); sL_nm = mk("sL_nm")
        sL_nrm = mk("sL_nrm"); sL_tr = mk("sL_tr"); sL_trev = mk("sL_trev")
        sL_y1st = mk("sL_y1st"); sL_yTst = mkl("sL_yTst", 2)
        sD_y1t = mk("sD_y1t"); sD_wg = mkl("sD_wg", 2); sD_wu = mkl("sD_wu", 2)
        sD_gu = mk("sD_gu"); sD_sg = mk("sD_sg"); sD_h = mk("sD_h")
        sD_wd = mkl("sD_wd", 4); sD_dmm = mk("sD_dmm"); sD_yr = mkl("sD_yr", 2)
        sD_r2 = mk("sD_r2"); sD_r2st = mkl("sD_r2st", 4)
        sM_in = mkl("sM_in", 2); sM_bn = mk("sM_bn"); sM_ag = mk("sM_ag")
        sM_sd = mk("sM_sd"); sM_rc = mk("sM_rc"); sM_nm = mk("sM_nm")
        sM_nrm = mk("sM_nrm"); sOUT = mk("sOUT")

        def bf(off, n):
            return POOL_BF[:, off:off + n]

        def f32(off, n):
            return POOL_F[:, off:off + n]

        cast_chunks = []
        for img_, bfd_, tot_ in [(wg_img, wg_bf, FC * 32 * 128),
                                 (wu_img, wu_bf, FC * 32 * 128),
                                 (wd_img, wd_bf, DBN * FC * 512)]:
            o_ = 0
            while o_ < tot_:
                ln_ = min(16384, tot_ - o_)
                cast_chunks.append((img_, bfd_, o_, ln_))
                o_ += ln_

        def emit_cast_chunk():
            if not cast_chunks:
                return
            img_, bfd_, o_, ln_ = cast_chunks.pop(0)
            def _pc(e, _d=bfd_[:, o_:o_ + ln_], _s=img_[:, o_:o_ + ln_]):
                e.dma_start(out=_d, in_=_s).then_inc(sPC.h, 16)
            em("g", _pc)
            sPC.n += 16

        def _init(e):
            e.dma_start(out=idf[:, :], in_=idf_in[:, :]).then_inc(sInit.h, 16)
        em("sy", _init)
        sInit.n += 16

        def _init_g(e):
            e.memset(ones_col[:, :], 1.0)
            e.memset(ones_row[:, :], 1.0)
            e.memset(epsb[:, :], 1e-5).then_inc(sInit.h, 1)
        em("g", _init_g)
        sInit.n += 1

        A_WQ, A_WK, A_WV = 0, 16384, 32768
        A_XC, A_QKE, A_VE = 49152, 65536, 67584

        wload_plan = [(wq_img, A_WQ, 0, sA_w0), (wk_img, A_WK, 0, sA_w0),
                      (wq_img, A_WQ, 1, sA_w1), (wk_img, A_WK, 1, sA_w1),
                      (wv_img, A_WV, 0, sA_w2), (wv_img, A_WV, 1, sA_w2)]
        def emit_wloads():
            for img, off, half, sm in wload_plan:
                def _ld(e, _d=bf(off + half * 8192, 8192),
                        _s=img[:, half * 8192:(half + 1) * 8192], _sm=sm):
                    e.dma_start(out=_d, in_=_s).then_inc(_sm.h, 16)
                em("g", _ld)
                sm.n += 16

        for tb in range(TBN):
            slot = tb % 2
            xc = bf(A_XC + slot * 8192, 8192)
            W("g", sA_mmv, 2 * (tb - 1))
            def _ldx(e, _d=xc, _s=xT_img[:, tb * 8192:(tb + 1) * 8192],
                     _sm=sA_x[slot]):
                e.dma_start(out=_d, in_=_s).then_inc(_sm.h, 16)
            em("g", _ldx)
            sA_x[slot].n += 16
            x_ready = sA_x[slot].n
            if tb == 0:
                emit_wloads()
            if tb % 2 == 0:
                emit_cast_chunk()

            W("t", sA_x[slot], x_ready)
            if tb == 0:
                W("t", sA_w0, 32)
            for b in range(8):
                if tb > 0:
                    W("t", sA_ev, 8 * (tb - 1) + b + 1)
                    if b < 2:
                        W("t", sA_vev, 2 * (tb - 1) + b + 1)
            for i in range(32):
                if tb == 0 and i == 16:
                    W("t", sA_w1, 32)
                for b in range(8):
                    woff = (A_WQ if b < 4 else A_WK) + i * 512 + (b % 4) * 128
                    def _mm(e, _w=bf(woff, 128), _x=xc[:, i * TBW:(i + 1) * TBW],
                            _o=ps[b][:, 0:TBW], _st=(i == 0), _sp=(i == 31)):
                        e.ldweights(_w)
                        ins = e.matmul(_o, _w, _x, start=_st, stop=_sp)
                        if _sp:
                            ins.then_inc(sA_mmq.h, 1)
                    em("t", _mm)
                    if i == 31:
                        sA_mmq.n += 1

            for b in range(8):
                W("sc", sA_mmq, 8 * tb + b + 1)
                W("sc", sA_qst[b], sA_qst[b].n)
                def _ev(e, _d=bf(A_QKE + b * TBW, TBW), _s=ps[b][:, 0:TBW]):
                    e.copy(_d, _s).then_inc(sA_ev.h, 1)
                em("sc", _ev)
                sA_ev.n += 1
                W("sy", sA_ev, 8 * tb + b + 1)
                dst = (qT_d if b < 4 else kT_d)[(b % 4) * DH:(b % 4 + 1) * DH,
                                                tb * TBW:(tb + 1) * TBW]
                def _st(e, _d=dst, _s=bf(A_QKE + b * TBW, TBW), _sm=sA_qst[b]):
                    e.dma_start(out=_d, in_=_s).then_inc(_sm.h, 16)
                em("sy", _st)
                sA_qst[b].n += 16

            if tb == 0:
                W("t", sA_w2, 32)
            for s_ in range(2):
                W("t", sA_ev, 8 * tb + s_ + 1)
            for i in range(32):
                for s_ in range(2):
                    def _mmv(e, _w=xc[:, i * TBW + s_ * 128:i * TBW + s_ * 128 + 128],
                             _x=bf(A_WV + i * 512, 512), _o=ps[s_][:, 0:512],
                             _st=(i == 0), _sp=(i == 31)):
                        e.ldweights(_w)
                        ins = e.matmul(_o, _w, _x, start=_st, stop=_sp)
                        if _sp:
                            ins.then_inc(sA_mmv.h, 1)
                    em("t", _mmv)
                    if i == 31:
                        sA_mmv.n += 1
            for s_ in range(2):
                W("sc", sA_mmv, 2 * tb + s_ + 1)
                W("sc", sA_vst[s_], sA_vst[s_].n)
                def _evv(e, _d=bf(A_VE + s_ * 512, 512), _s=ps[s_][:, 0:512]):
                    e.copy(_d, _s).then_inc(sA_vev.h, 1)
                em("sc", _evv)
                sA_vev.n += 1
                W("sy", sA_vev, 2 * tb + s_ + 1)
                kt_ = 2 * tb + s_
                def _stv(e, _d=v_d[:, kt_ * 512:(kt_ + 1) * 512],
                         _s=bf(A_VE + s_ * 512, 512), _sm=sA_vst[s_]):
                    e.dma_start(out=_d, in_=_s).then_inc(_sm.h, 16)
                em("sy", _stv)
                sA_vst[s_].n += 16

        B_KT, B_VH, B_QT, B_PT, B_SQ, B_OTE = 0, 16384, 33024, 49408, 53504, 54528
        B_VS = 55552
        barrier([(sA_qst[b], sA_qst[b].n) for b in range(8)] +
                [(sA_vst[s], sA_vst[s].n) for s in range(2)] +
                [(sInit, sInit.n)])

        def _msvh(e, _d=bf(B_VH, 16640)):
            e.memset(_d, 1.0).then_inc(sB_ms.h, 1)
        em("g", _msvh)
        sB_ms.n += 1
        sVst = sB_ms

        def kt_slot(h):
            return bf(B_KT + (h % 2) * 8192, 8192)

        def vh_slot(h):
            return bf(B_VH + (h % 2) * 8320, 8320)

        def qt_slot(h):
            return bf(B_QT + (h % 2) * 8192, 8192)

        def emit_head_loads(h):
            def _lk(e, _d=kt_slot(h), _s=kT_d[h * DH:(h + 1) * DH, :]):
                e.dma_start(out=_d, in_=_s).then_inc(kvs[h].h, 16)
            em("sy", _lk)
            kvs[h].n += 16
            def _lq(e, _d=qt_slot(h), _s=qT_d[h * DH:(h + 1) * DH, :]):
                e.dma_start(out=_d, in_=_s).then_inc(kvs[h].h, 16)
            em("sy", _lq)
            kvs[h].n += 16
            for si in range(4):
                k_ = 4 * h + si
                sslot = k_ % 2
                W("sy", sVcp, 16 * (k_ - 1))
                def _ls(e, _d=bf(B_VS + sslot * 8192, 8192),
                        _s=v_d[:, si * 8192:(si + 1) * 8192]):
                    e.dma_start(out=_d, in_=_s).then_inc(sVst.h, 16)
                em("sy", _ls)
                sVst.n += 16
                W("g", sVst, sVst.n)
                for cc in range(16):
                    kt = si * 16 + cc
                    def _cp(e, _d=vh_slot(h)[:, kt * 130:kt * 130 + 128],
                            _s=bf(B_VS + sslot * 8192 + cc * 512 + h * DH, 128)):
                        e.tensor_copy(_d, _s).then_inc(sVcp.h, 1)
                    em("g", _cp)
                    sVcp.n += 1

        emit_head_loads(0)
        emit_head_loads(1)
        pending_tr = None

        o_sb = [f32(qs * 128, 128) for qs in range(4)]
        rl_sb = f32(512, 8)
        bias_sb = f32(520, 32)

        for h in range(NH):
            hs = h % 2
            if pending_tr is not None:
                pending_tr()
                pending_tr = None
            for w_i in range(2):
                src_slot = kt_slot(h) if w_i == 0 else qt_slot(h)
                for cc in range(16):
                    ci = w_i * 16 + cc
                    gidx = 32 * h + ci
                    W("v", kvs[h], 32)
                    W("v", sB_nmm, gidx - 1)
                    def _sq(e, _d=bf(B_SQ + (ci % 2) * 512, 512),
                            _s=src_slot[:, cc * 512:(cc + 1) * 512]):
                        e.tensor_mul(_d, _s, _s).then_inc(sB_sq.h, 1)
                    em("v", _sq)
                    sB_sq.n += 1
                    W("t", sB_sq, gidx + 1)
                    W("t", sB_nred, gidx - 1)
                    if ci == 0:
                        W("t", sB_osc, 4 * QTN * h)
                        W("t", sB_rl, 4 * QTN * h)
                        W("t", sB_trev, 4 * QTN * h)
                    def _nm(e, _o=ps_o[2 + ci % 2][0:1, 0:512], _w=ones_col[:, :],
                            _x=bf(B_SQ + (ci % 2) * 512, 512)):
                        e.ldweights(_w)
                        e.matmul(_o, _w, _x, start=True, stop=True).then_inc(sB_nmm.h, 1)
                    em("t", _nm)
                    sB_nmm.n += 1
                    tblap = (ktbl if w_i == 0 else qtbl)[0:1, hs * 16 + cc:hs * 16 + cc + 1]
                    W("v", sB_nmm, gidx + 1)
                    def _nr(e, _d=tblap, _s=ps_o[2 + ci % 2][0:1, 0:512]):
                        e.reduce_max(_d, _s, axis=AX.X).then_inc(sB_nred.h, 1)
                    em("v", _nr)
                    sB_nred.n += 1
            W("v", sB_nred, 32 * (h + 1))
            def _c1(e, _d=kscal[0:1, hs:hs + 1], _s=ktbl[0:1, hs * 16:hs * 16 + 16]):
                e.reduce_max(_d, _s, axis=AX.X).then_inc(sB_ch.h, 1)
            em("v", _c1)
            sB_ch.n += 1
            W("v", sB_ch, sB_ch.n)
            def _c2(e, _d=tmp16[0:1, hs * 16:hs * 16 + 16],
                    _s=qtbl[0:1, hs * 16:hs * 16 + 16], _k=kscal[0:1, hs:hs + 1]):
                e.tensor_scalar(_d, _s, _k, None, ALU.mult).then_inc(sB_ch.h, 1)
            em("v", _c2)
            sB_ch.n += 1
            W("sc", sB_ch, sB_ch.n)
            def _c3(e, _d=tmp16b[0:1, hs * 16:hs * 16 + 16],
                    _s=tmp16[0:1, hs * 16:hs * 16 + 16]):
                e.activation(_d, _s, AF.Sqrt).then_inc(sB_chA.h, 1)
            em("sc", _c3)
            sB_chA.n += 1
            W("v", sB_chA, sB_chA.n)
            def _c4(e, _d=btbl[0:1, hs * 16:hs * 16 + 16],
                    _s=tmp16b[0:1, hs * 16:hs * 16 + 16]):
                e.tensor_scalar(_d, _s, -SCALE, None, ALU.mult).then_inc(sB_bt.h, 1)
            em("v", _c4)
            sB_bt.n += 1
            W("t", sB_bt, h + 1)
            def _bmh(e, _o=ps_o[3][:, 200:216], _w=ones_row[:, :],
                     _x=btbl[0:1, hs * 16:hs * 16 + 16]):
                e.ldweights(_w)
                e.matmul(_o, _w, _x, start=True, stop=True).then_inc(sB_bmm.h, 1)
            em("t", _bmh)
            sB_bmm.n += 1
            W("v", sB_bmm, sB_bmm.n)
            def _bch(e, _d=bias_sb[:, hs * 16:hs * 16 + 16], _s=ps_o[3][:, 200:216]):
                e.tensor_copy(_d, _s).then_inc(sB_bias.h, 1)
            em("v", _bch)
            sB_bias.n += 1

            for iu, u in enumerate(list(range(0, QTN, 2)) + list(range(1, QTN, 2))):
                ui = h * QTN + iu
                qv = qt_slot(h)[:, u * QTW:(u + 1) * QTW]
                if cast_chunks:
                    W("g", sB_smm, 64 * ui)
                    emit_cast_chunk()
                def emit_smm(c):
                    def _sm(e, _o=ps_pair[(c // 2) % 2][:, (c % 2) * 512:
                                                        (c % 2) * 512 + 512],
                            _w=kt_slot(h)[:, c * 128:(c + 1) * 128], _x=qv):
                        e.ldweights(_w)
                        e.matmul(_o, _w, _x, start=True, stop=True).then_inc(sB_smm.h, 1)
                    em("t", _sm)
                    sB_smm.n += 1

                W("t", kvs[h], 32)
                if u == 0:
                    W("t", sVcp, 64 * (h + 1))
                W("t", sB_exp, 32 * ui - 1)
                emit_smm(0)
                emit_smm(1)
                W("t", sB_exp, 32 * ui)
                emit_smm(2)
                emit_smm(3)
                if pending_tr is not None:
                    pending_tr()
                    pending_tr = None
                if h == NH - 1 and iu == QTN // 2:
                    W("g", sB_ost, sB_ost.n)
                    def _cca(e):
                        with allow_ici_all_to_all():
                            e.collective_compute(
                                "AllToAll", ALU.bypass,
                                replica_groups=[list(range(NCORE))],
                                ins=[a2a_inA.ap().opt()],
                                outs=[a2a_outA.ap().opt()],
                            ).then_inc(sCCa.h, 1)
                    em("g", _cca)
                    sCCa.n += 1
                for p in range(KCN // 2):
                    W("sc", sB_smm, 64 * ui + 2 * p + 2)
                    if p == 0:
                        W("sc", sB_bias, h + 1)
                    W("sc", sB_pvc, 64 * ui + 2 * p - 6)
                    def _ex(e, _d=bf(B_PT + (p % 4) * 1024, 1024),
                            _s=ps_pair[p % 2][:, :],
                            _b=bias_sb[:, hs * 16 + u:hs * 16 + u + 1]):
                        e.activation(_d, _s, AF.Exp, bias=_b, scale=SCALE) \
                            .then_inc(sB_exp.h, 1)
                    em("sc", _ex)
                    sB_exp.n += 1
                    W("t", sB_exp, 32 * ui + p + 1)
                    if p == 0:
                        W("t", sB_osc, 4 * ui)
                        W("t", sB_rl, 4 * ui)
                        W("t", sB_trev, 4 * ui)
                        if u == 0:
                            W("t", sB_nred, 32 * (h + 1))
                    for co in range(2):
                        c = 2 * p + co
                        for qs in range(4):
                            def _pv(e, _o=ps_o[qs][:, 0:129],
                                    _w=bf(B_PT + (p % 4) * 1024 + co * 512
                                          + qs * 128, 128),
                                    _x=vh_slot(h)[:, c * 130:c * 130 + 129],
                                    _st=(c == 0), _sp=(c == KCN - 1),
                                    _last=(qs == 3)):
                                e.ldweights(_w)
                                ins = e.matmul(_o, _w, _x, start=_st, stop=_sp)
                                if _last:
                                    ins.then_inc(sB_pvc.h, 1)
                            em("t", _pv)
                        sB_pvc.n += 1
                        if c + 4 < KCN:
                            emit_smm(c + 4)

                for qs in range(4):
                    W("v", sB_pvc, 64 * (ui + 1))
                    if ui >= 1 and qs == 0:
                        W("v", sB_osc, 4 * (ui - 1) + 4)
                    def _rc(e, _d=rl_sb[:, (ui % 2) * 4 + qs:(ui % 2) * 4 + qs + 1],
                            _s=ps_o[qs][:, 128:129]):
                        e.reciprocal(_d, _s).then_inc(sB_rl.h, 1)
                    em("v", _rc)
                    sB_rl.n += 1
                W("v", sB_rl, 4 * ui + 4)
                for qs in range(4):
                    if ui >= 1:
                        W("v", sB_tr, 4 * (ui - 1) + qs + 1)
                    def _os(e, _d=o_sb[qs], _s=ps_o[qs][:, 0:128],
                            _r=rl_sb[:, (ui % 2) * 4 + qs:(ui % 2) * 4 + qs + 1]):
                        e.tensor_scalar(_d, _s, _r, None, ALU.mult) \
                            .then_inc(sB_osc.h, 1)
                    em("v", _os)
                    sB_osc.n += 1
                def mk_pending_tr(_ui=ui, _u=u, _h=h):
                    def emit_trs():
                        oslot = B_OTE + (_ui % 2) * 512
                        for qs in range(4):
                            W("t", sB_osc, 4 * _ui + qs + 1)
                            W("t", sB_trev, 4 * (_ui - 1) + qs + 1)
                            def _tr(e, _o=ps_o[qs][:, 0:128], _s=o_sb[qs]):
                                e.transpose(_o, _s, idf[:, :]).then_inc(sB_tr.h, 1)
                            em("t", _tr)
                            sB_tr.n += 1
                        for qs in range(4):
                            W("v", sB_tr, 4 * _ui + qs + 1)
                            if qs == 0:
                                W("v", sB_ost, 16 * (_ui - 1))
                            def _te(e, _d=bf(oslot + qs * 128, 128),
                                    _s=ps_o[qs][:, 0:128]):
                                e.tensor_copy(_d, _s).then_inc(sB_trev.h, 1)
                            em("v", _te)
                            sB_trev.n += 1
                        r = (_u * QTW) // TC
                        a2a_half = a2a_inA if _u % 2 == 0 else a2a_inB
                        W("sy", sB_trev, 4 * _ui + 4)
                        def _so(e, _d=a2a_half[r * 512 + _h * 128:
                                               r * 512 + _h * 128 + 128, :],
                                _s=bf(oslot, 512)):
                            e.dma_start(out=_d, in_=_s).then_inc(sB_ost.h, 16)
                        em("sy", _so)
                        sB_ost.n += 16
                    return emit_trs
                pending_tr = mk_pending_tr()

            if h + 2 < NH:
                W("sy", sB_smm, KCN * QTN * (h + 1))
                W("sy", sB_pvc, KCN * QTN * (h + 1))
                W("sy", sB_sq, 32 * (h + 1))
                W("g", sB_pvc, KCN * QTN * (h + 1))
                emit_head_loads(h + 2)

        if pending_tr is not None:
            pending_tr()
            pending_tr = None
        otsrcA = a2a_outA.rearrange("(ic p) t -> p ic t", p=128)
        W("sy", sCCa, 1)
        for hf_ in range(2):
            def _loa(e, _d=bf(B_VS + hf_ * 8192, 8192).rearrange(
                         "p (ic t) -> p ic t", t=512),
                     _s=otsrcA[:, hf_ * 16:(hf_ + 1) * 16, :]):
                e.dma_start(out=_d, in_=_s).then_inc(sCCa.h, 16)
            em("sy", _loa)
            sCCa.n += 16
        W("g", sB_ost, sB_ost.n)
        def _cc(e):
            with allow_ici_all_to_all():
                e.collective_compute(
                    "AllToAll", ALU.bypass, replica_groups=[list(range(NCORE))],
                    ins=[a2a_inB.ap().opt()], outs=[a2a_outB.ap().opt()],
                ).then_inc(sCC.h, 1)
        em("g", _cc)
        sCC.n += 1

        C_OT, C_WO = 0, 32768
        barrier([(sCC, 1), (sB_trev, sB_trev.n), (sB_pvc, sB_pvc.n)])
        while cast_chunks:
            emit_cast_chunk()

        otsrcB = a2a_outB.rearrange("(ic p) t -> p ic t", p=128)
        for hf_ in range(2):
            def _lo(e, _d=bf(C_OT + hf_ * 8192, 8192).rearrange(
                        "p (ic t) -> p ic t", t=512),
                    _s=otsrcB[:, hf_ * 16:(hf_ + 1) * 16, :]):
                e.dma_start(out=_d, in_=_s).then_inc(sC_ot.h, 16)
            em("sy", _lo)
            sC_ot.n += 16
        for jbp_ in (0, 1, 2):
            def _pw1(e, _d=bf(C_WO + jbp_ * 8192, 8192),
                     _s=wo_img[:, jbp_ * 8192:(jbp_ + 1) * 8192], _sm=sC_wo[jbp_]):
                e.dma_start(out=_d, in_=_s).then_inc(_sm.h, 16)
            em("g", _pw1)
            sC_wo[jbp_].n += 16

        XR, R1E = 0, 4096
        r1_store_cnt = {}
        x_ready_tbl = {}
        def emit_xload(jb):
            W("sc", sC_ev, 8 * (jb - 1))
            def _lx(e, _d=f32(XR + (jb % 2) * 2048, 2048),
                    _s=x_img[:, jb * 2048:(jb + 1) * 2048], _sm=sC_x[jb % 2]):
                e.dma_start(out=_d, in_=_s).then_inc(_sm.h, 16)
            em("sc", _lx)
            sC_x[jb % 2].n += 16
            x_ready_tbl[jb] = sC_x[jb % 2].n
        emit_xload(0)
        emit_xload(1)
        for jb in range(JBN):
            wsl = jb % 3
            if jb >= 3:
                W("g", sC_mm, 8 * (jb - 2))
                def _lw(e, _d=bf(C_WO + wsl * 8192, 8192),
                        _s=wo_img[:, jb * 8192:(jb + 1) * 8192], _sm=sC_wo[wsl]):
                    e.dma_start(out=_d, in_=_s).then_inc(_sm.h, 16)
                em("g", _lw)
                sC_wo[wsl].n += 16
            wo_base = C_WO + wsl * 8192
            if jb + 2 < JBN:
                emit_xload(jb + 2)
            x_ready = x_ready_tbl[jb]

            W("t", sC_wo[wsl], sC_wo[wsl].n)
            for i in range(32):
                if jb == 0 and i == 0:
                    W("t", sCCa, 33)
                    W("t", sC_ot, 16)
                if jb == 0 and i == 16:
                    W("t", sC_ot, 32)
                for tp in range(8):
                    if i == 0 and jb > 0:
                        W("t", sC_ev, 8 * (jb - 1) + tp + 1)
                    otb = (B_VS + (i * 4 + tp) * 128) if tp < 4 else \
                        (C_OT + (i * 4 + tp - 4) * 128)
                    def _cm(e, _w=bf(otb, 128),
                            _x=bf(wo_base + i * JBW, JBW),
                            _o=ps[tp][:, 0:JBW], _st=(i == 0), _sp=(i == 31)):
                        e.ldweights(_w)
                        ins = e.matmul(_o, _w, _x, start=_st, stop=_sp)
                        if _sp:
                            ins.then_inc(sC_mm.h, 1)
                    em("t", _cm)
                    if i == 31:
                        sC_mm.n += 1
            for tp in range(8):
                W("v", sC_mm, 8 * jb + tp + 1)
                W("v", sC_x[jb % 2], x_ready)
                W("v", sC_st[tp % 4], r1_store_cnt.get((jb - 1, tp), 0))
                def _ce(e, _d=f32(R1E + tp * 256, 256),
                        _a=ps[tp][:, 0:JBW],
                        _b=f32(XR + (jb % 2) * 2048 + tp * 256, 256)):
                    e.tensor_add(_d, _a, _b).then_inc(sC_ev.h, 1)
                em("v", _ce)
                sC_ev.n += 1
                W("sy", sC_ev, 8 * jb + tp + 1)
                def _cs(e, _d=r1_d[tp * 128:(tp + 1) * 128, jb * JBW:(jb + 1) * JBW],
                        _s=f32(R1E + tp * 256, 256), _sm=sC_st[tp % 4]):
                    e.dma_start(out=_d, in_=_s).then_inc(_sm.h, 16)
                em("sy", _cs)
                sC_st[tp % 4].n += 16
                r1_store_cnt[(jb, tp)] = sC_st[tp % 4].n

        LN_IN, LN_Y1F, LN_ST = 0, 8192, 12288
        L_YTE = 0

        def emit_ln_phase(src_d, with_y1T, dst_plain, s_store, sems, tr_sems=None):
            (s_in, s_bn, s_ag, s_sd, s_rc, s_nm, s_nrm) = sems
            for tp in range(8):
                sl = tp % 2
                lnv = f32(LN_IN + sl * 4096, 4096)
                y1v = f32(LN_Y1F, 4096)
                st6 = LN_ST + sl * 64
                W("g", s_nrm, tp - 1)
                def _li(e, _d=lnv, _s=src_d[tp * 128:(tp + 1) * 128, :],
                        _sm=s_in[sl]):
                    e.dma_start(out=_d, in_=_s).then_inc(_sm.h, 16)
                em("g", _li)
                s_in[sl].n += 16
                in_ready = s_in[sl].n
                W("v", s_in[sl], in_ready)
                W("v", s_ag, tp - 1)
                for c in range(8):
                    def _bs(e, _d=f32(st6 + c * 6, 6),
                            _s=lnv[:, c * 512:(c + 1) * 512], _l=(c == 7)):
                        ins = e.bn_stats(_d, _s)
                        if _l:
                            ins.then_inc(s_bn.h, 1)
                    em("v", _bs)
                s_bn.n += 1
                W("v", s_bn, tp + 1)
                def _ba(e, _d=f32(st6 + 48, 2), _s=f32(st6, 48)):
                    e.bn_aggr(_d, _s).then_inc(s_ag.h, 1)
                em("v", _ba)
                s_ag.n += 1
                W("sc", s_ag, tp + 1)
                def _sd(e, _d=f32(st6 + 50, 1), _s=f32(st6 + 49, 1)):
                    e.activation(_d, _s, AF.Sqrt, bias=epsb[:, :]).then_inc(s_sd.h, 1)
                em("sc", _sd)
                s_sd.n += 1
                W("v", s_sd, tp + 1)
                def _rc(e, _d=f32(st6 + 51, 1), _s=f32(st6 + 50, 1)):
                    e.reciprocal(_d, _s).then_inc(s_rc.h, 1)
                em("v", _rc)
                s_rc.n += 1
                W("v", s_rc, s_rc.n)
                def _nm1(e, _d=f32(st6 + 52, 1), _a=f32(st6 + 48, 1),
                         _b=f32(st6 + 51, 1)):
                    e.tensor_mul(_d, _a, _b).then_inc(s_nm.h, 1)
                em("v", _nm1)
                s_nm.n += 1
                W("v", s_nm, s_nm.n)
                def _ng(e, _d=f32(st6 + 53, 1), _s=f32(st6 + 52, 1)):
                    e.tensor_scalar(_d, _s, -1.0, None, ALU.mult).then_inc(s_nm.h, 1)
                em("v", _ng)
                s_nm.n += 1
                W("v", s_nm, s_nm.n)
                if tr_sems is not None:
                    W("v", tr_sems[0], 32 * tp)
                W("v", s_store, 16 * tp)
                def _no(e, _d=y1v, _s=lnv, _r=f32(st6 + 51, 1), _m=f32(st6 + 53, 1)):
                    e.tensor_scalar(_d, _s, _r, _m, ALU.mult, ALU.add) \
                        .then_inc(s_nrm.h, 1)
                em("v", _no)
                s_nrm.n += 1
                W("sy", s_nrm, tp + 1)
                def _sp(e, _d=dst_plain[tp * 128:(tp + 1) * 128, :], _s=y1v):
                    e.dma_start(out=_d, in_=_s).then_inc(s_store.h, 16)
                em("sy", _sp)
                s_store.n += 16
                if with_y1T:
                    s_tr, s_trev, s_yTst = tr_sems
                    for ic in range(32):
                        W("t", s_nrm, tp + 1)
                        W("t", s_trev, 32 * tp + ic - 1)
                        def _t1(e, _o=ps[ic % 2][:, 0:128],
                                _s=y1v[:, ic * 128:(ic + 1) * 128]):
                            e.transpose(_o, _s, idf[:, :]).then_inc(s_tr.h, 1)
                        em("t", _t1)
                        s_tr.n += 1
                        W("sc", s_tr, 32 * tp + ic + 1)
                        if ic == 0:
                            W("sc", s_yTst[sl], s_yTst[sl].n)
                        def _t2(e, _d=bf(L_YTE + sl * 4096 + ic * 128, 128),
                                _s=ps[ic % 2][:, 0:128]):
                            e.copy(_d, _s).then_inc(s_trev.h, 1)
                        em("sc", _t2)
                        s_trev.n += 1
                    W("sy", s_trev, 32 * (tp + 1))
                    ydst = y1T_d.rearrange("(ic p) t -> p ic t", p=128)[
                        :, :, tp * 128:(tp + 1) * 128]
                    ysrc = bf(L_YTE + sl * 4096, 4096).rearrange(
                        "p (ic t) -> p ic t", t=128)
                    def _sy2(e, _d=ydst, _s=ysrc, _sm=s_yTst[sl]):
                        e.dma_start(out=_d, in_=_s).then_inc(_sm.h, 16)
                    em("sy", _sy2)
                    s_yTst[sl].n += 16

        barrier([(sC_st[i], sC_st[i].n) for i in range(4)])
        emit_ln_phase(r1_d, True, y1_d, sL_y1st,
                      (sL_in, sL_bn, sL_ag, sL_sd, sL_rc, sL_nm, sL_nrm),
                      (sL_tr, sL_trev, sL_yTst))

        D_H, D_Y1T, D_WG, D_WU, D_SG, D_WD = 0, 44032, 60416, 68608, 76800, 77824
        D_YR, D_R2E = 0, 8192
        barrier([(sL_yTst[0], sL_yTst[0].n), (sL_yTst[1], sL_yTst[1].n),
                 (sL_y1st, sL_y1st.n), (sPC, sPC.n)])

        wd_guard = {}
        for th in range(THN):
            W("sy", sD_gu, 172 * th)
            ysrc = y1T_d.rearrange("(ic p) t -> p ic t", p=128)[
                :, :, th * 512:(th + 1) * 512]
            ydst = bf(D_Y1T, 16384).rearrange("p (ic t) -> p ic t", t=512)
            def _ly(e, _d=ydst, _s=ysrc):
                e.dma_start(out=_d, in_=_s).then_inc(sD_y1t.h, 16)
            em("sy", _ly)
            sD_y1t.n += 16
            y1t_ready = sD_y1t.n

            for f in range(FC):
                fs = f % 2
                W("g", sD_gu, 172 * th + 2 * (f - 1))
                if th == 1 and f < 4:
                    W("g", sD_dmm, 4 * FC)
                def _lg(e, _d=bf(D_WG + fs * 4096, 4096),
                        _s=wg_bf[:, f * 4096:(f + 1) * 4096], _sm=sD_wg[fs]):
                    e.dma_start(out=_d, in_=_s).then_inc(_sm.h, 16)
                em("g", _lg)
                sD_wg[fs].n += 16
                g_ready = sD_wg[fs].n
                def _lu(e, _d=bf(D_WU + fs * 4096, 4096),
                        _s=wu_bf[:, f * 4096:(f + 1) * 4096], _sm=sD_wu[fs]):
                    e.dma_start(out=_d, in_=_s).then_inc(_sm.h, 16)
                em("g", _lu)
                sD_wu[fs].n += 16
                u_ready = sD_wu[fs].n

                if f == 0:
                    W("t", sD_y1t, y1t_ready)
                W("t", sD_wg[fs], g_ready)
                W("t", sD_sg, FC * th + f - 1)
                for i in range(32):
                    def _gm(e, _w=bf(D_WG + fs * 4096 + i * 128, 128),
                            _x=bf(D_Y1T + i * 512, 512),
                            _o=ps[fs * 2][:, 0:512], _st=(i == 0), _sp=(i == 31)):
                        e.ldweights(_w)
                        ins = e.matmul(_o, _w, _x, start=_st, stop=_sp)
                        if _sp:
                            ins.then_inc(sD_gu.h, 1)
                    em("t", _gm)
                sD_gu.n += 1
                W("t", sD_wu[fs], u_ready)
                W("t", sD_h, FC * th + f - 1)
                for i in range(32):
                    def _um(e, _w=bf(D_WU + fs * 4096 + i * 128, 128),
                            _x=bf(D_Y1T + i * 512, 512),
                            _o=ps[fs * 2 + 1][:, 0:512], _st=(i == 0), _sp=(i == 31)):
                        e.ldweights(_w)
                        ins = e.matmul(_o, _w, _x, start=_st, stop=_sp)
                        if _sp:
                            ins.then_inc(sD_gu.h, 1)
                    em("t", _um)
                sD_gu.n += 1
                W("sc", sD_gu, 172 * th + 2 * f + 1)
                W("sc", sD_h, FC * th + f - 1)
                def _si(e, _d=bf(D_SG + fs * 512, 512), _s=ps[fs * 2][:, 0:512]):
                    e.activation(_d, _s, AF.Silu).then_inc(sD_sg.h, 1)
                em("sc", _si)
                sD_sg.n += 1
                W("v", sD_gu, 172 * th + 2 * f + 2)
                W("v", sD_sg, FC * th + f + 1)
                if th == 1 and f == 0:
                    W("v", sD_dmm, 4 * FC)
                def _hm(e, _d=bf(D_H + f * 512, 512),
                        _a=ps[fs * 2 + 1][:, 0:512], _b=bf(D_SG + fs * 512, 512)):
                    e.tensor_tensor(_d, _a, _b, ALU.mult).then_inc(sD_h.h, 1)
                em("v", _hm)
                sD_h.n += 1

            for dbp in range(DBP):
                dbs = dbp % 2
                W("sy", sD_r2, 4 * (th * DBP + dbp - 1))
                ysrc3 = y1_d.rearrange("(tq tp p) j -> tq p tp j", tq=2, p=128)[
                    th, :, :, dbp * 1024:(dbp + 1) * 1024]
                ydst3 = f32(D_YR + dbs * 4096, 4096).rearrange(
                    "p (tp j) -> p tp j", j=1024)
                def _lyr(e, _d=ydst3, _s=ysrc3, _sm=sD_yr[dbs]):
                    e.dma_start(out=_d, in_=_s).then_inc(_sm.h, 16)
                em("sy", _lyr)
                sD_yr[dbs].n += 16
                yr_ready = sD_yr[dbs].n
                NGRP = (FC + 3) // 4
                if dbp == 0:
                    W("t", sD_h, FC * (th + 1))
                for grp in range(NGRP):
                    f0 = grp * 4
                    nf = min(4, FC - f0)
                    gidx = (th * DBP + dbp) * NGRP + grp
                    gs = gidx % 4
                    if dbp == 0 and grp < 4:
                        W("g", sD_gu, 172 * (th + 1))
                    W("g", sD_dmm, wd_guard.get(gidx, 0))
                    def _lwd(e, _d=bf(D_WG + gs * 4096, nf * 1024),
                             _s=wd_bf[:, (dbp * FC + f0) * 1024:
                                      (dbp * FC + f0 + nf) * 1024],
                             _sm=sD_wd[gs]):
                        e.dma_start(out=_d, in_=_s).then_inc(_sm.h, 16)
                    em("g", _lwd)
                    sD_wd[gs].n += 16
                    wd_ready = sD_wd[gs].n
                    W("t", sD_wd[gs], wd_ready)
                    for fo in range(nf):
                        f = f0 + fo
                        if f == 0:
                            W("t", sD_r2, 4 * (th * DBP + dbp))
                        for tp4 in range(4):
                            for hf in range(2):
                                def _dm(e, _w=bf(D_H + f * 512 + tp4 * 128, 128),
                                        _x=bf(D_WG + gs * 4096 + fo * 1024
                                              + hf * 512, 512),
                                        _o=ps[2 * tp4 + hf][:, 0:512],
                                        _st=(f == 0), _sp=(f == FC - 1),
                                        _ld=(hf == 0),
                                        _last=(tp4 == 3 and hf == 1)):
                                    if _ld:
                                        e.ldweights(_w)
                                    ins = e.matmul(_o, _w, _x, start=_st, stop=_sp)
                                    if _last:
                                        ins.then_inc(sD_dmm.h, 1)
                                em("t", _dm)
                        sD_dmm.n += 1
                    wd_guard[gidx + 4] = sD_dmm.n
                for tp4 in range(4):
                    W("v", sD_dmm, th * 4 * FC + dbp * FC + FC)
                    W("v", sD_yr[dbs], yr_ready)
                    W("v", sD_r2st[tp4], sD_r2st[tp4].n)
                    for hf in range(2):
                        def _de(e, _d=f32(D_R2E + tp4 * 1024 + hf * 512, 512),
                                _a=ps[2 * tp4 + hf][:, 0:512],
                                _b=f32(D_YR + dbs * 4096 + tp4 * 1024
                                       + hf * 512, 512),
                                _last=(hf == 1)):
                            ins = e.tensor_add(_d, _a, _b)
                            if _last:
                                ins.then_inc(sD_r2.h, 1)
                        em("v", _de)
                    sD_r2.n += 1
                    W("sy", sD_r2, 4 * (th * DBP + dbp) + tp4 + 1)
                    def _ds(e, _d=r2_d[th * 512 + tp4 * 128:th * 512 + tp4 * 128 + 128,
                                       dbp * 1024:(dbp + 1) * 1024],
                            _s=f32(D_R2E + tp4 * 1024, 1024), _sm=sD_r2st[tp4]):
                        e.dma_start(out=_d, in_=_s).then_inc(_sm.h, 16)
                    em("sy", _ds)
                    sD_r2st[tp4].n += 16

        barrier([(sD_r2st[i], sD_r2st[i].n) for i in range(4)])
        emit_ln_phase(r2_d, False, out, sOUT,
                      (sM_in, sM_bn, sM_ag, sM_sd, sM_rc, sM_nm, sM_nrm))

        W("sy", sOUT, sOUT.n)

        if collect is not None:
            collect["Q"] = Q
            return nc

        with nc.Block() as block:
            @block.tensor
            def _(e):
                for fn in Q["t"]:
                    fn(e)

            @block.scalar
            def _(e):
                for fn in Q["sc"]:
                    fn(e)

            @block.vector
            def _(e):
                for fn in Q["v"]:
                    fn(e)

            @block.gpsimd
            def _(e):
                for fn in Q["g"]:
                    fn(e)

            @block.sync
            def _(e):
                for fn in Q["sy"]:
                    fn(e)

    return nc


_NC_CACHE = None


def _get_nc():
    global _NC_CACHE
    if _NC_CACHE is None:
        _NC_CACHE = build_program()
    return _NC_CACHE


def kernel(L0_attn_x, L0_attn_Wq, L0_attn_Wk, L0_attn_Wv, L0_attn_Wo,
           L0_ffn_W_gate, L0_ffn_W_up, L0_ffn_W_down):
    x2d = np.ascontiguousarray(np.asarray(L0_attn_x, dtype=np.float32)[0])
    Wq = np.asarray(L0_attn_Wq, dtype=np.float32)
    Wk = np.asarray(L0_attn_Wk, dtype=np.float32)
    Wv = np.asarray(L0_attn_Wv, dtype=np.float32)
    Wo = np.asarray(L0_attn_Wo, dtype=np.float32)
    Wg = np.asarray(L0_ffn_W_gate, dtype=np.float32)
    Wu = np.asarray(L0_ffn_W_up, dtype=np.float32)
    Wd = np.asarray(L0_ffn_W_down, dtype=np.float32)

    xT_img = np.ascontiguousarray(
        x2d.reshape(TBN, TBW, 32, 128).transpose(3, 0, 2, 1)).reshape(128, -1)
    wo_img = np.ascontiguousarray(
        Wo.reshape(JBN, JBW, 32, 128).transpose(3, 0, 2, 1)).reshape(128, -1)
    wg_img = np.ascontiguousarray(
        Wg.reshape(FC, 128, 32, 128).transpose(3, 0, 2, 1)).reshape(128, -1)
    wu_img = np.ascontiguousarray(
        Wu.reshape(FC, 128, 32, 128).transpose(3, 0, 2, 1)).reshape(128, -1)
    wd_img = np.ascontiguousarray(
        Wd.reshape(DBP, 1024, FC, 128).transpose(3, 0, 2, 1)).reshape(128, -1)
    idf = np.eye(128, dtype=np.float32)

    in_maps = []
    for c in range(NCORE):
        jr = slice(512 * c, 512 * (c + 1))
        wq_i = np.ascontiguousarray(
            Wq[jr].reshape(512, 32, 128).transpose(2, 1, 0)).reshape(128, -1)
        wk_i = np.ascontiguousarray(
            Wk[jr].reshape(512, 32, 128).transpose(2, 1, 0)).reshape(128, -1)
        wv_i = np.ascontiguousarray(
            Wv[jr].reshape(512, 32, 128).transpose(2, 1, 0)).reshape(128, -1)
        x_i = np.ascontiguousarray(
            x2d[TC * c:TC * (c + 1)].reshape(8, 128, JBN, JBW)
            .transpose(1, 2, 0, 3)).reshape(128, -1)
        in_maps.append({
            "xT_img": xT_img, "wq_img": wq_i, "wk_img": wk_i, "wv_img": wv_i,
            "wo_img": wo_img, "wg_img": wg_img, "wu_img": wu_img,
            "wd_img": wd_img, "x_img": x_i, "idf_in": idf,
        })

    nc = _get_nc()
    res = run_bass_kernel_spmd(nc, in_maps, core_ids=list(range(NCORE)))
    full = np.concatenate([res.results[c]["out"] for c in range(NCORE)], axis=0)
    return full.reshape(1, S, D)


# version 15 (final)
# speedup vs baseline: 1.0170x; 1.0170x over previous
import numpy as np
import concourse.bass as bass
import concourse.mybir as mybir
from concourse.bass_utils import run_bass_kernel_spmd
from concourse.aot_env import allow_ici_all_to_all
from contextlib import ExitStack

F32 = mybir.dt.float32
BF16 = mybir.dt.bfloat16
AF = mybir.ActivationFunctionType
ALU = mybir.AluOpType
AX = mybir.AxisListType

NCORE = 8
D = 4096
S = 8192
TC = S // NCORE
NH = 4
DH = 128
FF = 11008
FC = 86
TBN, TBW = 32, 256
QTN, QTW = 16, 512
KCN = 64
JBN, JBW = 16, 256
DBN = 8
DBP = 4
THN = 2
SCALE = 0.08838834764831845

ENGS = ("t", "sc", "v", "g", "sy")


class Sem:
    __slots__ = ("h", "n")

    def __init__(self, h):
        self.h = h
        self.n = 0


def build_program(collect=None):
    nc = bass.Bass("TRN2", target_bir_lowering=False, debug=False)

    xT_img = nc.dram_tensor("xT_img", [128, TBN * 32 * TBW], F32, kind="ExternalInput")
    wq_img = nc.dram_tensor("wq_img", [128, 32 * 512], F32, kind="ExternalInput")
    wk_img = nc.dram_tensor("wk_img", [128, 32 * 512], F32, kind="ExternalInput")
    wv_img = nc.dram_tensor("wv_img", [128, 32 * 512], F32, kind="ExternalInput")
    wo_img = nc.dram_tensor("wo_img", [128, JBN * 32 * JBW], F32, kind="ExternalInput")
    wg_img = nc.dram_tensor("wg_img", [128, FC * 32 * 128], F32, kind="ExternalInput")
    wu_img = nc.dram_tensor("wu_img", [128, FC * 32 * 128], F32, kind="ExternalInput")
    wd_img = nc.dram_tensor("wd_img", [128, DBN * FC * 512], F32, kind="ExternalInput")
    x_img = nc.dram_tensor("x_img", [128, JBN * 8 * JBW], F32, kind="ExternalInput")
    idf_in = nc.dram_tensor("idf_in", [128, 128], F32, kind="ExternalInput")
    out = nc.dram_tensor("out", [TC, D], F32, kind="ExternalOutput")

    kT_d = nc.dram_tensor("kT_d", [NH * DH, S], BF16)
    qT_d = nc.dram_tensor("qT_d", [NH * DH, S], BF16)
    v_d = nc.dram_tensor("v_d", [128, KCN * NH * DH], BF16)
    a2a_inA = nc.dram_tensor("a2a_inA", [D, TC // 2], BF16)
    a2a_inB = nc.dram_tensor("a2a_inB", [D, TC // 2], BF16)
    a2a_outA = nc.dram_tensor("a2a_outA", [D, TC // 2], BF16)
    a2a_outB = nc.dram_tensor("a2a_outB", [D, TC // 2], BF16)
    r1_d = nc.dram_tensor("r1_d", [TC, D], F32)
    y1_d = nc.dram_tensor("y1_d", [TC, D], F32)
    y1T_d = nc.dram_tensor("y1T_d", [D, TC], BF16)
    r2_d = nc.dram_tensor("r2_d", [TC, D], F32)
    wg_bf = nc.dram_tensor("wg_bf", [128, FC * 32 * 128], BF16)
    wu_bf = nc.dram_tensor("wu_bf", [128, FC * 32 * 128], BF16)
    wd_bf = nc.dram_tensor("wd_bf", [128, DBN * FC * 512], BF16)

    Q = {k: [] for k in ENGS}

    def em(q, fn):
        Q[q].append(fn)

    def W(q, sem, val):
        if val > 0:
            em(q, lambda e, _s=sem.h, _v=val: e.wait_ge(_s, _v))

    def barrier(conds):
        for q in ENGS:
            for sem, val in conds:
                W(q, sem, val)

    ctx = ExitStack()
    with ctx:
        POOL_BF = ctx.enter_context(nc.sbuf_tensor("pool_bf", [128, 78848], BF16))
        POOL_F = ctx.enter_context(nc.sbuf_tensor("pool_f", [128, 12544], F32))
        idf = ctx.enter_context(nc.sbuf_tensor("idf", [128, 128], F32))
        ones_col = ctx.enter_context(nc.sbuf_tensor("ones_col", [128, 1], BF16))
        ones_row = ctx.enter_context(nc.sbuf_tensor("ones_row", [1, 128], BF16))
        epsb = ctx.enter_context(nc.sbuf_tensor("epsb", [128, 1], F32))
        ktbl = ctx.enter_context(nc.sbuf_tensor("ktbl", [1, 32], F32))
        qtbl = ctx.enter_context(nc.sbuf_tensor("qtbl", [1, 32], F32))
        kscal = ctx.enter_context(nc.sbuf_tensor("kscal", [1, 2], F32))
        tmp16 = ctx.enter_context(nc.sbuf_tensor("tmp16", [1, 32], F32))
        tmp16b = ctx.enter_context(nc.sbuf_tensor("tmp16b", [1, 32], F32))
        btbl = ctx.enter_context(nc.sbuf_tensor("btbl", [1, 32], BF16))

        ps_a = ctx.enter_context(nc.psum_tensor("ps_a", [128, 1024], F32))
        ps_b = ctx.enter_context(nc.psum_tensor("ps_b", [128, 1024], F32))
        ps_o = [ctx.enter_context(nc.psum_tensor(f"ps_o{i}", [128, 512], F32))
                for i in range(4)]
        ps = [ps_a[:, 0:512], ps_a[:, 512:1024],
              ps_b[:, 0:512], ps_b[:, 512:1024],
              ps_o[0][:, :], ps_o[1][:, :], ps_o[2][:, :], ps_o[3][:, :]]
        ps_pair = [ps_a, ps_b]

        def mk(name):
            return Sem(ctx.enter_context(nc.semaphore(name)))

        def mkl(name, n):
            return [mk(f"{name}{i}") for i in range(n)]

        sInit = mk("sInit")
        sPC = mk("sPC")
        sA_x = mkl("sA_x", 2)
        sA_w0 = mk("sA_w0"); sA_w1 = mk("sA_w1"); sA_w2 = mk("sA_w2")
        sA_mmq = mk("sA_mmq"); sA_mmv = mk("sA_mmv")
        sA_ev = mk("sA_ev"); sA_vev = mk("sA_vev")
        sA_qst = mkl("sA_qst", 8); sA_vst = mkl("sA_vst", 2)
        sB_ms = mk("sB_ms"); kvs = mkl("sB_kv", 4)
        sVcp = mk("sB_vcp")
        sB_sq = mk("sB_sq"); sB_nmm = mk("sB_nmm"); sB_nred = mk("sB_nred")
        sB_ch = mk("sB_ch"); sB_chA = mk("sB_chA"); sB_bt = mk("sB_bt")
        sB_bmm = mk("sB_bmm"); sB_bias = mk("sB_bias")
        sB_smm = mk("sB_smm"); sB_exp = mk("sB_exp"); sB_pvc = mk("sB_pvc")
        sB_rl = mk("sB_rl"); sB_osc = mk("sB_osc")
        sB_tr = mk("sB_tr"); sB_trev = mk("sB_trev"); sB_ost = mk("sB_ost")
        sCC = mk("sCC"); sCCa = mk("sCCa")
        sC_ot = mk("sC_ot"); sC_wo = mkl("sC_wo", 3); sC_x = mkl("sC_x", 2)
        sC_mm = mk("sC_mm"); sC_ev = mk("sC_ev"); sC_st = mkl("sC_st", 4)
        sL_in = mkl("sL_in", 2); sL_bn = mk("sL_bn"); sL_ag = mk("sL_ag")
        sL_sd = mk("sL_sd"); sL_rc = mk("sL_rc"); sL_nm = mk("sL_nm")
        sL_nrm = mk("sL_nrm"); sL_tr = mk("sL_tr"); sL_trev = mk("sL_trev")
        sL_y1st = mk("sL_y1st"); sL_yTst = mkl("sL_yTst", 2)
        sD_y1t = mk("sD_y1t"); sD_wg = mkl("sD_wg", 2); sD_wu = mkl("sD_wu", 2)
        sD_gu = mk("sD_gu"); sD_sg = mk("sD_sg"); sD_h = mk("sD_h")
        sD_wd = mkl("sD_wd", 4); sD_dmm = mk("sD_dmm"); sD_yr = mkl("sD_yr", 2)
        sD_r2 = mk("sD_r2"); sD_r2st = mkl("sD_r2st", 4)
        sM_in = mkl("sM_in", 2); sM_bn = mk("sM_bn"); sM_ag = mk("sM_ag")
        sM_sd = mk("sM_sd"); sM_rc = mk("sM_rc"); sM_nm = mk("sM_nm")
        sM_nrm = mk("sM_nrm"); sOUT = mk("sOUT")

        def bf(off, n):
            return POOL_BF[:, off:off + n]

        def f32(off, n):
            return POOL_F[:, off:off + n]

        cast_chunks = []
        for img_, bfd_, tot_ in [(wg_img, wg_bf, FC * 32 * 128),
                                 (wu_img, wu_bf, FC * 32 * 128),
                                 (wd_img, wd_bf, DBN * FC * 512)]:
            o_ = 0
            while o_ < tot_:
                ln_ = min(16384, tot_ - o_)
                cast_chunks.append((img_, bfd_, o_, ln_))
                o_ += ln_

        def emit_cast_chunk():
            if not cast_chunks:
                return
            img_, bfd_, o_, ln_ = cast_chunks.pop(0)
            def _pc(e, _d=bfd_[:, o_:o_ + ln_], _s=img_[:, o_:o_ + ln_]):
                e.dma_start(out=_d, in_=_s).then_inc(sPC.h, 16)
            em("g", _pc)
            sPC.n += 16

        def _init(e):
            e.dma_start(out=idf[:, :], in_=idf_in[:, :]).then_inc(sInit.h, 16)
        em("sy", _init)
        sInit.n += 16

        def _init_g(e):
            e.memset(ones_col[:, :], 1.0)
            e.memset(ones_row[:, :], 1.0)
            e.memset(epsb[:, :], 1e-5).then_inc(sInit.h, 1)
        em("g", _init_g)
        sInit.n += 1

        A_WQ, A_WK, A_WV = 0, 16384, 32768
        A_XC, A_QKE, A_VE = 49152, 65536, 67584

        wload_plan = [(wq_img, A_WQ, 0, sA_w0), (wk_img, A_WK, 0, sA_w0),
                      (wq_img, A_WQ, 1, sA_w1), (wk_img, A_WK, 1, sA_w1),
                      (wv_img, A_WV, 0, sA_w2), (wv_img, A_WV, 1, sA_w2)]
        def emit_wloads():
            for img, off, half, sm in wload_plan:
                def _ld(e, _d=bf(off + half * 8192, 8192),
                        _s=img[:, half * 8192:(half + 1) * 8192], _sm=sm):
                    e.dma_start(out=_d, in_=_s).then_inc(_sm.h, 16)
                em("g", _ld)
                sm.n += 16

        for tb in range(TBN):
            slot = tb % 2
            xc = bf(A_XC + slot * 8192, 8192)
            x_base = sA_x[slot].n
            for xq in range(4):
                if xq == 0:
                    W("g", sA_mmv, 2 * (tb - 1))
                def _ldx(e, _d=bf(A_XC + slot * 8192 + xq * 2048, 2048),
                         _s=xT_img[:, tb * 8192 + xq * 2048:
                                   tb * 8192 + (xq + 1) * 2048],
                         _sm=sA_x[slot]):
                    e.dma_start(out=_d, in_=_s).then_inc(_sm.h, 16)
                em("g", _ldx)
                sA_x[slot].n += 16
            if tb == 0:
                emit_wloads()
            if tb % 2 == 0:
                emit_cast_chunk()

            if tb == 0:
                W("t", sA_w0, 32)
            for b in range(8):
                if tb > 0:
                    W("t", sA_ev, 8 * (tb - 1) + b + 1)
                    if b < 2:
                        W("t", sA_vev, 2 * (tb - 1) + b + 1)
            for i in range(32):
                if i % 8 == 0:
                    W("t", sA_x[slot], x_base + 16 * (i // 8 + 1))
                if tb == 0 and i == 16:
                    W("t", sA_w1, 32)
                for b in range(8):
                    woff = (A_WQ if b < 4 else A_WK) + i * 512 + (b % 4) * 128
                    def _mm(e, _w=bf(woff, 128), _x=xc[:, i * TBW:(i + 1) * TBW],
                            _o=ps[b][:, 0:TBW], _st=(i == 0), _sp=(i == 31)):
                        e.ldweights(_w)
                        ins = e.matmul(_o, _w, _x, start=_st, stop=_sp)
                        if _sp:
                            ins.then_inc(sA_mmq.h, 1)
                    em("t", _mm)
                    if i == 31:
                        sA_mmq.n += 1

            for b in range(8):
                W("sc", sA_mmq, 8 * tb + b + 1)
                W("sc", sA_qst[b], sA_qst[b].n)
                def _ev(e, _d=bf(A_QKE + b * TBW, TBW), _s=ps[b][:, 0:TBW]):
                    e.copy(_d, _s).then_inc(sA_ev.h, 1)
                em("sc", _ev)
                sA_ev.n += 1
                W("sy", sA_ev, 8 * tb + b + 1)
                dst = (qT_d if b < 4 else kT_d)[(b % 4) * DH:(b % 4 + 1) * DH,
                                                tb * TBW:(tb + 1) * TBW]
                def _st(e, _d=dst, _s=bf(A_QKE + b * TBW, TBW), _sm=sA_qst[b]):
                    e.dma_start(out=_d, in_=_s).then_inc(_sm.h, 16)
                em("sy", _st)
                sA_qst[b].n += 16

            if tb == 0:
                W("t", sA_w2, 32)
            for s_ in range(2):
                W("t", sA_ev, 8 * tb + s_ + 1)
            for i in range(32):
                for s_ in range(2):
                    def _mmv(e, _w=xc[:, i * TBW + s_ * 128:i * TBW + s_ * 128 + 128],
                             _x=bf(A_WV + i * 512, 512), _o=ps[s_][:, 0:512],
                             _st=(i == 0), _sp=(i == 31)):
                        e.ldweights(_w)
                        ins = e.matmul(_o, _w, _x, start=_st, stop=_sp)
                        if _sp:
                            ins.then_inc(sA_mmv.h, 1)
                    em("t", _mmv)
                    if i == 31:
                        sA_mmv.n += 1
            for s_ in range(2):
                W("sc", sA_mmv, 2 * tb + s_ + 1)
                W("sc", sA_vst[s_], sA_vst[s_].n)
                def _evv(e, _d=bf(A_VE + s_ * 512, 512), _s=ps[s_][:, 0:512]):
                    e.copy(_d, _s).then_inc(sA_vev.h, 1)
                em("sc", _evv)
                sA_vev.n += 1
                W("sy", sA_vev, 2 * tb + s_ + 1)
                kt_ = 2 * tb + s_
                def _stv(e, _d=v_d[:, kt_ * 512:(kt_ + 1) * 512],
                         _s=bf(A_VE + s_ * 512, 512), _sm=sA_vst[s_]):
                    e.dma_start(out=_d, in_=_s).then_inc(_sm.h, 16)
                em("sy", _stv)
                sA_vst[s_].n += 16

        B_KT, B_VH, B_QT, B_PT, B_SQ, B_OTE = 0, 16384, 33024, 49408, 53504, 54528
        B_VS = 55552
        barrier([(sA_qst[b], sA_qst[b].n) for b in range(8)] +
                [(sA_vst[s], sA_vst[s].n) for s in range(2)] +
                [(sInit, sInit.n)])

        def _msvh(e, _d=bf(B_VH, 16640)):
            e.memset(_d, 1.0).then_inc(sB_ms.h, 1)
        em("g", _msvh)
        sB_ms.n += 1
        sVst = sB_ms

        def kt_slot(h):
            return bf(B_KT + (h % 2) * 8192, 8192)

        def vh_slot(h):
            return bf(B_VH + (h % 2) * 8320, 8320)

        def qt_slot(h):
            return bf(B_QT + (h % 2) * 8192, 8192)

        def emit_head_loads(h):
            def _lk(e, _d=kt_slot(h), _s=kT_d[h * DH:(h + 1) * DH, :]):
                e.dma_start(out=_d, in_=_s).then_inc(kvs[h].h, 16)
            em("sy", _lk)
            kvs[h].n += 16
            def _lq(e, _d=qt_slot(h), _s=qT_d[h * DH:(h + 1) * DH, :]):
                e.dma_start(out=_d, in_=_s).then_inc(kvs[h].h, 16)
            em("sy", _lq)
            kvs[h].n += 16
            for si in range(4):
                k_ = 4 * h + si
                sslot = k_ % 2
                W("sy", sVcp, 16 * (k_ - 1))
                def _ls(e, _d=bf(B_VS + sslot * 8192, 8192),
                        _s=v_d[:, si * 8192:(si + 1) * 8192]):
                    e.dma_start(out=_d, in_=_s).then_inc(sVst.h, 16)
                em("sy", _ls)
                sVst.n += 16
                W("g", sVst, sVst.n)
                for cc in range(16):
                    kt = si * 16 + cc
                    def _cp(e, _d=vh_slot(h)[:, kt * 130:kt * 130 + 128],
                            _s=bf(B_VS + sslot * 8192 + cc * 512 + h * DH, 128)):
                        e.tensor_copy(_d, _s).then_inc(sVcp.h, 1)
                    em("g", _cp)
                    sVcp.n += 1

        emit_head_loads(0)
        emit_head_loads(1)
        pending_tr = None

        o_sb = [f32(qs * 128, 128) for qs in range(4)]
        rl_sb = f32(512, 8)
        bias_sb = f32(520, 32)

        for h in range(NH):
            hs = h % 2
            if pending_tr is not None:
                pending_tr()
                pending_tr = None
            for w_i in range(2):
                src_slot = kt_slot(h) if w_i == 0 else qt_slot(h)
                for cc in range(16):
                    ci = w_i * 16 + cc
                    gidx = 32 * h + ci
                    W("v", kvs[h], 32)
                    W("v", sB_nmm, gidx - 1)
                    def _sq(e, _d=bf(B_SQ + (ci % 2) * 512, 512),
                            _s=src_slot[:, cc * 512:(cc + 1) * 512]):
                        e.tensor_mul(_d, _s, _s).then_inc(sB_sq.h, 1)
                    em("v", _sq)
                    sB_sq.n += 1
                    W("t", sB_sq, gidx + 1)
                    W("t", sB_nred, gidx - 1)
                    if ci == 0:
                        W("t", sB_osc, 4 * QTN * h)
                        W("t", sB_rl, 4 * QTN * h)
                        W("t", sB_trev, 4 * QTN * h)
                    def _nm(e, _o=ps_o[2 + ci % 2][0:1, 0:512], _w=ones_col[:, :],
                            _x=bf(B_SQ + (ci % 2) * 512, 512)):
                        e.ldweights(_w)
                        e.matmul(_o, _w, _x, start=True, stop=True).then_inc(sB_nmm.h, 1)
                    em("t", _nm)
                    sB_nmm.n += 1
                    tblap = (ktbl if w_i == 0 else qtbl)[0:1, hs * 16 + cc:hs * 16 + cc + 1]
                    W("v", sB_nmm, gidx + 1)
                    def _nr(e, _d=tblap, _s=ps_o[2 + ci % 2][0:1, 0:512]):
                        e.reduce_max(_d, _s, axis=AX.X).then_inc(sB_nred.h, 1)
                    em("v", _nr)
                    sB_nred.n += 1
            W("v", sB_nred, 32 * (h + 1))
            def _c1(e, _d=kscal[0:1, hs:hs + 1], _s=ktbl[0:1, hs * 16:hs * 16 + 16]):
                e.reduce_max(_d, _s, axis=AX.X).then_inc(sB_ch.h, 1)
            em("v", _c1)
            sB_ch.n += 1
            W("v", sB_ch, sB_ch.n)
            def _c2(e, _d=tmp16[0:1, hs * 16:hs * 16 + 16],
                    _s=qtbl[0:1, hs * 16:hs * 16 + 16], _k=kscal[0:1, hs:hs + 1]):
                e.tensor_scalar(_d, _s, _k, None, ALU.mult).then_inc(sB_ch.h, 1)
            em("v", _c2)
            sB_ch.n += 1
            W("sc", sB_ch, sB_ch.n)
            def _c3(e, _d=tmp16b[0:1, hs * 16:hs * 16 + 16],
                    _s=tmp16[0:1, hs * 16:hs * 16 + 16]):
                e.activation(_d, _s, AF.Sqrt).then_inc(sB_chA.h, 1)
            em("sc", _c3)
            sB_chA.n += 1
            W("v", sB_chA, sB_chA.n)
            def _c4(e, _d=btbl[0:1, hs * 16:hs * 16 + 16],
                    _s=tmp16b[0:1, hs * 16:hs * 16 + 16]):
                e.tensor_scalar(_d, _s, -SCALE, None, ALU.mult).then_inc(sB_bt.h, 1)
            em("v", _c4)
            sB_bt.n += 1
            W("t", sB_bt, h + 1)
            def _bmh(e, _o=ps_o[3][:, 200:216], _w=ones_row[:, :],
                     _x=btbl[0:1, hs * 16:hs * 16 + 16]):
                e.ldweights(_w)
                e.matmul(_o, _w, _x, start=True, stop=True).then_inc(sB_bmm.h, 1)
            em("t", _bmh)
            sB_bmm.n += 1
            W("v", sB_bmm, sB_bmm.n)
            def _bch(e, _d=bias_sb[:, hs * 16:hs * 16 + 16], _s=ps_o[3][:, 200:216]):
                e.tensor_copy(_d, _s).then_inc(sB_bias.h, 1)
            em("v", _bch)
            sB_bias.n += 1

            for iu, u in enumerate(list(range(0, QTN, 2)) + list(range(1, QTN, 2))):
                ui = h * QTN + iu
                qv = qt_slot(h)[:, u * QTW:(u + 1) * QTW]
                if cast_chunks:
                    W("g", sB_smm, 64 * ui)
                    emit_cast_chunk()
                def emit_smm(c):
                    def _sm(e, _o=ps_pair[(c // 2) % 2][:, (c % 2) * 512:
                                                        (c % 2) * 512 + 512],
                            _w=kt_slot(h)[:, c * 128:(c + 1) * 128], _x=qv):
                        e.ldweights(_w)
                        e.matmul(_o, _w, _x, start=True, stop=True).then_inc(sB_smm.h, 1)
                    em("t", _sm)
                    sB_smm.n += 1

                W("t", kvs[h], 32)
                if u == 0:
                    W("t", sVcp, 64 * (h + 1))
                W("t", sB_exp, 32 * ui - 1)
                emit_smm(0)
                emit_smm(1)
                W("t", sB_exp, 32 * ui)
                emit_smm(2)
                emit_smm(3)
                if pending_tr is not None:
                    pending_tr()
                    pending_tr = None
                if h == NH - 1 and iu == QTN // 2:
                    W("g", sB_ost, sB_ost.n)
                    def _cca(e):
                        with allow_ici_all_to_all():
                            e.collective_compute(
                                "AllToAll", ALU.bypass,
                                replica_groups=[list(range(NCORE))],
                                ins=[a2a_inA.ap().opt()],
                                outs=[a2a_outA.ap().opt()],
                            ).then_inc(sCCa.h, 1)
                    em("g", _cca)
                    sCCa.n += 1
                for p in range(KCN // 2):
                    W("sc", sB_smm, 64 * ui + 2 * p + 2)
                    if p == 0:
                        W("sc", sB_bias, h + 1)
                    W("sc", sB_pvc, 64 * ui + 2 * p - 6)
                    def _ex(e, _d=bf(B_PT + (p % 4) * 1024, 1024),
                            _s=ps_pair[p % 2][:, :],
                            _b=bias_sb[:, hs * 16 + u:hs * 16 + u + 1]):
                        e.activation(_d, _s, AF.Exp, bias=_b, scale=SCALE) \
                            .then_inc(sB_exp.h, 1)
                    em("sc", _ex)
                    sB_exp.n += 1
                    W("t", sB_exp, 32 * ui + p + 1)
                    if p == 0:
                        W("t", sB_osc, 4 * ui)
                        W("t", sB_rl, 4 * ui)
                        W("t", sB_trev, 4 * ui)
                        if u == 0:
                            W("t", sB_nred, 32 * (h + 1))
                    for co in range(2):
                        c = 2 * p + co
                        for qs in range(4):
                            def _pv(e, _o=ps_o[qs][:, 0:129],
                                    _w=bf(B_PT + (p % 4) * 1024 + co * 512
                                          + qs * 128, 128),
                                    _x=vh_slot(h)[:, c * 130:c * 130 + 129],
                                    _st=(c == 0), _sp=(c == KCN - 1),
                                    _last=(qs == 3)):
                                e.ldweights(_w)
                                ins = e.matmul(_o, _w, _x, start=_st, stop=_sp)
                                if _last:
                                    ins.then_inc(sB_pvc.h, 1)
                            em("t", _pv)
                        sB_pvc.n += 1
                        if c + 4 < KCN:
                            emit_smm(c + 4)

                for qs in range(4):
                    W("v", sB_pvc, 64 * (ui + 1))
                    if ui >= 1 and qs == 0:
                        W("v", sB_osc, 4 * (ui - 1) + 4)
                    def _rc(e, _d=rl_sb[:, (ui % 2) * 4 + qs:(ui % 2) * 4 + qs + 1],
                            _s=ps_o[qs][:, 128:129]):
                        e.reciprocal(_d, _s).then_inc(sB_rl.h, 1)
                    em("v", _rc)
                    sB_rl.n += 1
                W("v", sB_rl, 4 * ui + 4)
                for qs in range(4):
                    if ui >= 1:
                        W("v", sB_tr, 4 * (ui - 1) + qs + 1)
                    def _os(e, _d=o_sb[qs], _s=ps_o[qs][:, 0:128],
                            _r=rl_sb[:, (ui % 2) * 4 + qs:(ui % 2) * 4 + qs + 1]):
                        e.tensor_scalar(_d, _s, _r, None, ALU.mult) \
                            .then_inc(sB_osc.h, 1)
                    em("v", _os)
                    sB_osc.n += 1
                def mk_pending_tr(_ui=ui, _u=u, _h=h):
                    def emit_trs():
                        oslot = B_OTE + (_ui % 2) * 512
                        for qs in range(4):
                            W("t", sB_osc, 4 * _ui + qs + 1)
                            W("t", sB_trev, 4 * (_ui - 1) + qs + 1)
                            def _tr(e, _o=ps_o[qs][:, 0:128], _s=o_sb[qs]):
                                e.transpose(_o, _s, idf[:, :]).then_inc(sB_tr.h, 1)
                            em("t", _tr)
                            sB_tr.n += 1
                        for qs in range(4):
                            W("v", sB_tr, 4 * _ui + qs + 1)
                            if qs == 0:
                                W("v", sB_ost, 16 * (_ui - 1))
                            def _te(e, _d=bf(oslot + qs * 128, 128),
                                    _s=ps_o[qs][:, 0:128]):
                                e.tensor_copy(_d, _s).then_inc(sB_trev.h, 1)
                            em("v", _te)
                            sB_trev.n += 1
                        r = (_u * QTW) // TC
                        a2a_half = a2a_inA if _u % 2 == 0 else a2a_inB
                        W("sy", sB_trev, 4 * _ui + 4)
                        def _so(e, _d=a2a_half[r * 512 + _h * 128:
                                               r * 512 + _h * 128 + 128, :],
                                _s=bf(oslot, 512)):
                            e.dma_start(out=_d, in_=_s).then_inc(sB_ost.h, 16)
                        em("sy", _so)
                        sB_ost.n += 16
                    return emit_trs
                pending_tr = mk_pending_tr()

            if h + 2 < NH:
                W("sy", sB_smm, KCN * QTN * (h + 1))
                W("sy", sB_pvc, KCN * QTN * (h + 1))
                W("sy", sB_sq, 32 * (h + 1))
                W("g", sB_pvc, KCN * QTN * (h + 1))
                emit_head_loads(h + 2)

        if pending_tr is not None:
            pending_tr()
            pending_tr = None
        otsrcA = a2a_outA.rearrange("(ic p) t -> p ic t", p=128)
        W("sy", sCCa, 1)
        for hf_ in range(2):
            def _loa(e, _d=bf(B_VS + hf_ * 8192, 8192).rearrange(
                         "p (ic t) -> p ic t", t=512),
                     _s=otsrcA[:, hf_ * 16:(hf_ + 1) * 16, :]):
                e.dma_start(out=_d, in_=_s).then_inc(sCCa.h, 16)
            em("sy", _loa)
            sCCa.n += 16
        W("g", sB_ost, sB_ost.n)
        def _cc(e):
            with allow_ici_all_to_all():
                e.collective_compute(
                    "AllToAll", ALU.bypass, replica_groups=[list(range(NCORE))],
                    ins=[a2a_inB.ap().opt()], outs=[a2a_outB.ap().opt()],
                ).then_inc(sCC.h, 1)
        em("g", _cc)
        sCC.n += 1

        C_OT, C_WO = 0, 32768
        barrier([(sCC, 1), (sB_trev, sB_trev.n), (sB_pvc, sB_pvc.n)])
        while cast_chunks:
            emit_cast_chunk()

        otsrcB = a2a_outB.rearrange("(ic p) t -> p ic t", p=128)
        for hf_ in range(2):
            def _lo(e, _d=bf(C_OT + hf_ * 8192, 8192).rearrange(
                        "p (ic t) -> p ic t", t=512),
                    _s=otsrcB[:, hf_ * 16:(hf_ + 1) * 16, :]):
                e.dma_start(out=_d, in_=_s).then_inc(sC_ot.h, 16)
            em("sy", _lo)
            sC_ot.n += 16
        for jbp_ in (0, 1, 2):
            def _pw1(e, _d=bf(C_WO + jbp_ * 8192, 8192),
                     _s=wo_img[:, jbp_ * 8192:(jbp_ + 1) * 8192], _sm=sC_wo[jbp_]):
                e.dma_start(out=_d, in_=_s).then_inc(_sm.h, 16)
            em("g", _pw1)
            sC_wo[jbp_].n += 16

        XR, R1E = 0, 4096
        r1_store_cnt = {}
        x_ready_tbl = {}
        def emit_xload(jb):
            W("sc", sC_ev, 8 * (jb - 1))
            def _lx(e, _d=f32(XR + (jb % 2) * 2048, 2048),
                    _s=x_img[:, jb * 2048:(jb + 1) * 2048], _sm=sC_x[jb % 2]):
                e.dma_start(out=_d, in_=_s).then_inc(_sm.h, 16)
            em("sc", _lx)
            sC_x[jb % 2].n += 16
            x_ready_tbl[jb] = sC_x[jb % 2].n
        emit_xload(0)
        emit_xload(1)
        for jb in range(JBN):
            wsl = jb % 3
            if jb >= 3:
                W("g", sC_mm, 8 * (jb - 2))
                def _lw(e, _d=bf(C_WO + wsl * 8192, 8192),
                        _s=wo_img[:, jb * 8192:(jb + 1) * 8192], _sm=sC_wo[wsl]):
                    e.dma_start(out=_d, in_=_s).then_inc(_sm.h, 16)
                em("g", _lw)
                sC_wo[wsl].n += 16
            wo_base = C_WO + wsl * 8192
            if jb + 2 < JBN:
                emit_xload(jb + 2)
            x_ready = x_ready_tbl[jb]

            W("t", sC_wo[wsl], sC_wo[wsl].n)
            for i in range(32):
                if jb == 0 and i == 0:
                    W("t", sCCa, 33)
                    W("t", sC_ot, 16)
                if jb == 0 and i == 16:
                    W("t", sC_ot, 32)
                for tp in range(8):
                    if i == 0 and jb > 0:
                        W("t", sC_ev, 8 * (jb - 1) + tp + 1)
                    otb = (B_VS + (i * 4 + tp) * 128) if tp < 4 else \
                        (C_OT + (i * 4 + tp - 4) * 128)
                    def _cm(e, _w=bf(otb, 128),
                            _x=bf(wo_base + i * JBW, JBW),
                            _o=ps[tp][:, 0:JBW], _st=(i == 0), _sp=(i == 31)):
                        e.ldweights(_w)
                        ins = e.matmul(_o, _w, _x, start=_st, stop=_sp)
                        if _sp:
                            ins.then_inc(sC_mm.h, 1)
                    em("t", _cm)
                    if i == 31:
                        sC_mm.n += 1
            for tp in range(8):
                W("v", sC_mm, 8 * jb + tp + 1)
                W("v", sC_x[jb % 2], x_ready)
                W("v", sC_st[tp % 4], r1_store_cnt.get((jb - 1, tp), 0))
                def _ce(e, _d=f32(R1E + tp * 256, 256),
                        _a=ps[tp][:, 0:JBW],
                        _b=f32(XR + (jb % 2) * 2048 + tp * 256, 256)):
                    e.tensor_add(_d, _a, _b).then_inc(sC_ev.h, 1)
                em("v", _ce)
                sC_ev.n += 1
                W("sy", sC_ev, 8 * jb + tp + 1)
                def _cs(e, _d=r1_d[tp * 128:(tp + 1) * 128, jb * JBW:(jb + 1) * JBW],
                        _s=f32(R1E + tp * 256, 256), _sm=sC_st[tp % 4]):
                    e.dma_start(out=_d, in_=_s).then_inc(_sm.h, 16)
                em("sy", _cs)
                sC_st[tp % 4].n += 16
                r1_store_cnt[(jb, tp)] = sC_st[tp % 4].n

        LN_IN, LN_Y1F, LN_ST = 0, 8192, 12288
        L_YTE = 0

        def emit_ln_phase(src_d, with_y1T, dst_plain, s_store, sems, tr_sems=None,
                          hook=None):
            (s_in, s_bn, s_ag, s_sd, s_rc, s_nm, s_nrm) = sems
            for tp in range(8):
                if tp == 5 and hook is not None:
                    hook()
                sl = tp % 2
                lnv = f32(LN_IN + sl * 4096, 4096)
                y1v = f32(LN_Y1F, 4096)
                st6 = LN_ST + sl * 64
                W("g", s_nrm, tp - 1)
                def _li(e, _d=lnv, _s=src_d[tp * 128:(tp + 1) * 128, :],
                        _sm=s_in[sl]):
                    e.dma_start(out=_d, in_=_s).then_inc(_sm.h, 16)
                em("g", _li)
                s_in[sl].n += 16
                in_ready = s_in[sl].n
                W("v", s_in[sl], in_ready)
                W("v", s_ag, tp - 1)
                for c in range(8):
                    def _bs(e, _d=f32(st6 + c * 6, 6),
                            _s=lnv[:, c * 512:(c + 1) * 512], _l=(c == 7)):
                        ins = e.bn_stats(_d, _s)
                        if _l:
                            ins.then_inc(s_bn.h, 1)
                    em("v", _bs)
                s_bn.n += 1
                W("v", s_bn, tp + 1)
                def _ba(e, _d=f32(st6 + 48, 2), _s=f32(st6, 48)):
                    e.bn_aggr(_d, _s).then_inc(s_ag.h, 1)
                em("v", _ba)
                s_ag.n += 1
                W("sc", s_ag, tp + 1)
                def _sd(e, _d=f32(st6 + 50, 1), _s=f32(st6 + 49, 1)):
                    e.activation(_d, _s, AF.Sqrt, bias=epsb[:, :]).then_inc(s_sd.h, 1)
                em("sc", _sd)
                s_sd.n += 1
                W("v", s_sd, tp + 1)
                def _rc(e, _d=f32(st6 + 51, 1), _s=f32(st6 + 50, 1)):
                    e.reciprocal(_d, _s).then_inc(s_rc.h, 1)
                em("v", _rc)
                s_rc.n += 1
                W("v", s_rc, s_rc.n)
                def _nm1(e, _d=f32(st6 + 52, 1), _a=f32(st6 + 48, 1),
                         _b=f32(st6 + 51, 1)):
                    e.tensor_mul(_d, _a, _b).then_inc(s_nm.h, 1)
                em("v", _nm1)
                s_nm.n += 1
                W("v", s_nm, s_nm.n)
                def _ng(e, _d=f32(st6 + 53, 1), _s=f32(st6 + 52, 1)):
                    e.tensor_scalar(_d, _s, -1.0, None, ALU.mult).then_inc(s_nm.h, 1)
                em("v", _ng)
                s_nm.n += 1
                W("v", s_nm, s_nm.n)
                if tr_sems is not None:
                    W("v", tr_sems[0], 32 * tp)
                W("v", s_store, 16 * tp)
                def _no(e, _d=y1v, _s=lnv, _r=f32(st6 + 51, 1), _m=f32(st6 + 53, 1)):
                    e.tensor_scalar(_d, _s, _r, _m, ALU.mult, ALU.add) \
                        .then_inc(s_nrm.h, 1)
                em("v", _no)
                s_nrm.n += 1
                W("sy", s_nrm, tp + 1)
                def _sp(e, _d=dst_plain[tp * 128:(tp + 1) * 128, :], _s=y1v):
                    e.dma_start(out=_d, in_=_s).then_inc(s_store.h, 16)
                em("sy", _sp)
                s_store.n += 16
                if with_y1T:
                    s_tr, s_trev, s_yTst = tr_sems
                    for ic in range(32):
                        W("t", s_nrm, tp + 1)
                        W("t", s_trev, 32 * tp + ic - 1)
                        def _t1(e, _o=ps[ic % 2][:, 0:128],
                                _s=y1v[:, ic * 128:(ic + 1) * 128]):
                            e.transpose(_o, _s, idf[:, :]).then_inc(s_tr.h, 1)
                        em("t", _t1)
                        s_tr.n += 1
                        W("sc", s_tr, 32 * tp + ic + 1)
                        if ic == 0:
                            W("sc", s_yTst[sl], s_yTst[sl].n)
                        def _t2(e, _d=bf(L_YTE + sl * 4096 + ic * 128, 128),
                                _s=ps[ic % 2][:, 0:128]):
                            e.copy(_d, _s).then_inc(s_trev.h, 1)
                        em("sc", _t2)
                        s_trev.n += 1
                    W("sy", s_trev, 32 * (tp + 1))
                    ydst = y1T_d.rearrange("(ic p) t -> p ic t", p=128)[
                        :, :, tp * 128:(tp + 1) * 128]
                    ysrc = bf(L_YTE + sl * 4096, 4096).rearrange(
                        "p (ic t) -> p ic t", t=128)
                    def _sy2(e, _d=ydst, _s=ysrc, _sm=s_yTst[sl]):
                        e.dma_start(out=_d, in_=_s).then_inc(_sm.h, 16)
                    em("sy", _sy2)
                    s_yTst[sl].n += 16

        barrier([(sC_st[i], sC_st[i].n) for i in range(4)])

        D_H, D_Y1T, D_WG, D_WU, D_SG, D_WD = 0, 44032, 60416, 68608, 76800, 77824

        def d_preload_hook():
            W("g", sC_mm, 8 * JBN)
            W("g", sL_yTst[0], 32)
            W("g", sL_yTst[1], 32)
            W("g", sPC, 16 * 23)
            ysrc0 = y1T_d.rearrange("(ic p) t -> p ic t", p=128)[:, :, 0:512]
            ydst0 = bf(D_Y1T, 16384).rearrange("p (ic t) -> p ic t", t=512)
            def _ly0(e, _d=ydst0, _s=ysrc0):
                e.dma_start(out=_d, in_=_s).then_inc(sD_y1t.h, 16)
            em("g", _ly0)
            sD_y1t.n += 16
            for f_ in range(2):
                def _lg0(e, _d=bf(D_WG + f_ * 4096, 4096),
                         _s=wg_bf[:, f_ * 4096:(f_ + 1) * 4096], _sm=sD_wg[f_]):
                    e.dma_start(out=_d, in_=_s).then_inc(_sm.h, 16)
                em("g", _lg0)
                sD_wg[f_].n += 16
                def _lu0(e, _d=bf(D_WU + f_ * 4096, 4096),
                         _s=wu_bf[:, f_ * 4096:(f_ + 1) * 4096], _sm=sD_wu[f_]):
                    e.dma_start(out=_d, in_=_s).then_inc(_sm.h, 16)
                em("g", _lu0)
                sD_wu[f_].n += 16

        emit_ln_phase(r1_d, True, y1_d, sL_y1st,
                      (sL_in, sL_bn, sL_ag, sL_sd, sL_rc, sL_nm, sL_nrm),
                      (sL_tr, sL_trev, sL_yTst), hook=d_preload_hook)

        D_YR, D_R2E = 0, 8192
        barrier([(sL_yTst[0], sL_yTst[0].n), (sL_yTst[1], sL_yTst[1].n),
                 (sL_y1st, sL_y1st.n), (sPC, sPC.n)])

        wd_guard = {}
        for th in range(THN):
            if th > 0:
                W("sy", sD_gu, 172 * th)
                ysrc = y1T_d.rearrange("(ic p) t -> p ic t", p=128)[
                    :, :, th * 512:(th + 1) * 512]
                ydst = bf(D_Y1T, 16384).rearrange("p (ic t) -> p ic t", t=512)
                def _ly(e, _d=ydst, _s=ysrc):
                    e.dma_start(out=_d, in_=_s).then_inc(sD_y1t.h, 16)
                em("sy", _ly)
                sD_y1t.n += 16
            y1t_ready = 16 * (th + 1)

            for f in range(FC):
                fs = f % 2
                if th > 0 or f >= 2:
                    W("g", sD_gu, 172 * th + 2 * (f - 1))
                    if th == 1 and f < 4:
                        W("g", sD_dmm, 4 * FC)
                    def _lg(e, _d=bf(D_WG + fs * 4096, 4096),
                            _s=wg_bf[:, f * 4096:(f + 1) * 4096], _sm=sD_wg[fs]):
                        e.dma_start(out=_d, in_=_s).then_inc(_sm.h, 16)
                    em("g", _lg)
                    sD_wg[fs].n += 16
                    def _lu(e, _d=bf(D_WU + fs * 4096, 4096),
                            _s=wu_bf[:, f * 4096:(f + 1) * 4096], _sm=sD_wu[fs]):
                        e.dma_start(out=_d, in_=_s).then_inc(_sm.h, 16)
                    em("g", _lu)
                    sD_wu[fs].n += 16
                g_ready = sD_wg[fs].n
                u_ready = sD_wu[fs].n

                if f == 0:
                    W("t", sD_y1t, y1t_ready)
                W("t", sD_wg[fs], g_ready)
                W("t", sD_sg, FC * th + f - 1)
                for i in range(32):
                    def _gm(e, _w=bf(D_WG + fs * 4096 + i * 128, 128),
                            _x=bf(D_Y1T + i * 512, 512),
                            _o=ps[fs * 2][:, 0:512], _st=(i == 0), _sp=(i == 31)):
                        e.ldweights(_w)
                        ins = e.matmul(_o, _w, _x, start=_st, stop=_sp)
                        if _sp:
                            ins.then_inc(sD_gu.h, 1)
                    em("t", _gm)
                sD_gu.n += 1
                W("t", sD_wu[fs], u_ready)
                W("t", sD_h, FC * th + f - 1)
                for i in range(32):
                    def _um(e, _w=bf(D_WU + fs * 4096 + i * 128, 128),
                            _x=bf(D_Y1T + i * 512, 512),
                            _o=ps[fs * 2 + 1][:, 0:512], _st=(i == 0), _sp=(i == 31)):
                        e.ldweights(_w)
                        ins = e.matmul(_o, _w, _x, start=_st, stop=_sp)
                        if _sp:
                            ins.then_inc(sD_gu.h, 1)
                    em("t", _um)
                sD_gu.n += 1
                W("sc", sD_gu, 172 * th + 2 * f + 1)
                W("sc", sD_h, FC * th + f - 1)
                def _si(e, _d=bf(D_SG + fs * 512, 512), _s=ps[fs * 2][:, 0:512]):
                    e.activation(_d, _s, AF.Silu).then_inc(sD_sg.h, 1)
                em("sc", _si)
                sD_sg.n += 1
                W("v", sD_gu, 172 * th + 2 * f + 2)
                W("v", sD_sg, FC * th + f + 1)
                if th == 1 and f == 0:
                    W("v", sD_dmm, 4 * FC)
                def _hm(e, _d=bf(D_H + f * 512, 512),
                        _a=ps[fs * 2 + 1][:, 0:512], _b=bf(D_SG + fs * 512, 512)):
                    e.tensor_tensor(_d, _a, _b, ALU.mult).then_inc(sD_h.h, 1)
                em("v", _hm)
                sD_h.n += 1

            for dbp in range(DBP):
                dbs = dbp % 2
                W("sy", sD_r2, 4 * (th * DBP + dbp - 1))
                ysrc3 = y1_d.rearrange("(tq tp p) j -> tq p tp j", tq=2, p=128)[
                    th, :, :, dbp * 1024:(dbp + 1) * 1024]
                ydst3 = f32(D_YR + dbs * 4096, 4096).rearrange(
                    "p (tp j) -> p tp j", j=1024)
                def _lyr(e, _d=ydst3, _s=ysrc3, _sm=sD_yr[dbs]):
                    e.dma_start(out=_d, in_=_s).then_inc(_sm.h, 16)
                em("sy", _lyr)
                sD_yr[dbs].n += 16
                yr_ready = sD_yr[dbs].n
                NGRP = (FC + 3) // 4
                if dbp == 0:
                    W("t", sD_h, FC * (th + 1))
                for grp in range(NGRP):
                    f0 = grp * 4
                    nf = min(4, FC - f0)
                    gidx = (th * DBP + dbp) * NGRP + grp
                    gs = gidx % 4
                    if dbp == 0 and grp < 4:
                        W("g", sD_gu, 172 * (th + 1))
                    W("g", sD_dmm, wd_guard.get(gidx, 0))
                    def _lwd(e, _d=bf(D_WG + gs * 4096, nf * 1024),
                             _s=wd_bf[:, (dbp * FC + f0) * 1024:
                                      (dbp * FC + f0 + nf) * 1024],
                             _sm=sD_wd[gs]):
                        e.dma_start(out=_d, in_=_s).then_inc(_sm.h, 16)
                    em("g", _lwd)
                    sD_wd[gs].n += 16
                    wd_ready = sD_wd[gs].n
                    W("t", sD_wd[gs], wd_ready)
                    for fo in range(nf):
                        f = f0 + fo
                        if f == 0:
                            W("t", sD_r2, 4 * (th * DBP + dbp))
                        for tp4 in range(4):
                            for hf in range(2):
                                def _dm(e, _w=bf(D_H + f * 512 + tp4 * 128, 128),
                                        _x=bf(D_WG + gs * 4096 + fo * 1024
                                              + hf * 512, 512),
                                        _o=ps[2 * tp4 + hf][:, 0:512],
                                        _st=(f == 0), _sp=(f == FC - 1),
                                        _ld=(hf == 0),
                                        _last=(tp4 == 3 and hf == 1)):
                                    if _ld:
                                        e.ldweights(_w)
                                    ins = e.matmul(_o, _w, _x, start=_st, stop=_sp)
                                    if _last:
                                        ins.then_inc(sD_dmm.h, 1)
                                em("t", _dm)
                        sD_dmm.n += 1
                    wd_guard[gidx + 4] = sD_dmm.n
                for tp4 in range(4):
                    W("v", sD_dmm, th * 4 * FC + dbp * FC + FC)
                    W("v", sD_yr[dbs], yr_ready)
                    W("v", sD_r2st[tp4], sD_r2st[tp4].n)
                    for hf in range(2):
                        def _de(e, _d=f32(D_R2E + tp4 * 1024 + hf * 512, 512),
                                _a=ps[2 * tp4 + hf][:, 0:512],
                                _b=f32(D_YR + dbs * 4096 + tp4 * 1024
                                       + hf * 512, 512),
                                _last=(hf == 1)):
                            ins = e.tensor_add(_d, _a, _b)
                            if _last:
                                ins.then_inc(sD_r2.h, 1)
                        em("v", _de)
                    sD_r2.n += 1
                    W("sy", sD_r2, 4 * (th * DBP + dbp) + tp4 + 1)
                    def _ds(e, _d=r2_d[th * 512 + tp4 * 128:th * 512 + tp4 * 128 + 128,
                                       dbp * 1024:(dbp + 1) * 1024],
                            _s=f32(D_R2E + tp4 * 1024, 1024), _sm=sD_r2st[tp4]):
                        e.dma_start(out=_d, in_=_s).then_inc(_sm.h, 16)
                    em("sy", _ds)
                    sD_r2st[tp4].n += 16

        barrier([(sD_r2st[i], sD_r2st[i].n) for i in range(4)])
        emit_ln_phase(r2_d, False, out, sOUT,
                      (sM_in, sM_bn, sM_ag, sM_sd, sM_rc, sM_nm, sM_nrm))

        W("sy", sOUT, sOUT.n)

        if collect is not None:
            collect["Q"] = Q
            return nc

        with nc.Block() as block:
            @block.tensor
            def _(e):
                for fn in Q["t"]:
                    fn(e)

            @block.scalar
            def _(e):
                for fn in Q["sc"]:
                    fn(e)

            @block.vector
            def _(e):
                for fn in Q["v"]:
                    fn(e)

            @block.gpsimd
            def _(e):
                for fn in Q["g"]:
                    fn(e)

            @block.sync
            def _(e):
                for fn in Q["sy"]:
                    fn(e)

    return nc


_NC_CACHE = None


def _get_nc():
    global _NC_CACHE
    if _NC_CACHE is None:
        _NC_CACHE = build_program()
    return _NC_CACHE


def kernel(L0_attn_x, L0_attn_Wq, L0_attn_Wk, L0_attn_Wv, L0_attn_Wo,
           L0_ffn_W_gate, L0_ffn_W_up, L0_ffn_W_down):
    x2d = np.ascontiguousarray(np.asarray(L0_attn_x, dtype=np.float32)[0])
    Wq = np.asarray(L0_attn_Wq, dtype=np.float32)
    Wk = np.asarray(L0_attn_Wk, dtype=np.float32)
    Wv = np.asarray(L0_attn_Wv, dtype=np.float32)
    Wo = np.asarray(L0_attn_Wo, dtype=np.float32)
    Wg = np.asarray(L0_ffn_W_gate, dtype=np.float32)
    Wu = np.asarray(L0_ffn_W_up, dtype=np.float32)
    Wd = np.asarray(L0_ffn_W_down, dtype=np.float32)

    xT_img = np.ascontiguousarray(
        x2d.reshape(TBN, TBW, 32, 128).transpose(3, 0, 2, 1)).reshape(128, -1)
    wo_img = np.ascontiguousarray(
        Wo.reshape(JBN, JBW, 32, 128).transpose(3, 0, 2, 1)).reshape(128, -1)
    wg_img = np.ascontiguousarray(
        Wg.reshape(FC, 128, 32, 128).transpose(3, 0, 2, 1)).reshape(128, -1)
    wu_img = np.ascontiguousarray(
        Wu.reshape(FC, 128, 32, 128).transpose(3, 0, 2, 1)).reshape(128, -1)
    wd_img = np.ascontiguousarray(
        Wd.reshape(DBP, 1024, FC, 128).transpose(3, 0, 2, 1)).reshape(128, -1)
    idf = np.eye(128, dtype=np.float32)

    in_maps = []
    for c in range(NCORE):
        jr = slice(512 * c, 512 * (c + 1))
        wq_i = np.ascontiguousarray(
            Wq[jr].reshape(512, 32, 128).transpose(2, 1, 0)).reshape(128, -1)
        wk_i = np.ascontiguousarray(
            Wk[jr].reshape(512, 32, 128).transpose(2, 1, 0)).reshape(128, -1)
        wv_i = np.ascontiguousarray(
            Wv[jr].reshape(512, 32, 128).transpose(2, 1, 0)).reshape(128, -1)
        x_i = np.ascontiguousarray(
            x2d[TC * c:TC * (c + 1)].reshape(8, 128, JBN, JBW)
            .transpose(1, 2, 0, 3)).reshape(128, -1)
        in_maps.append({
            "xT_img": xT_img, "wq_img": wq_i, "wk_img": wk_i, "wv_img": wv_i,
            "wo_img": wo_img, "wg_img": wg_img, "wu_img": wu_img,
            "wd_img": wd_img, "x_img": x_i, "idf_in": idf,
        })

    nc = _get_nc()
    res = run_bass_kernel_spmd(nc, in_maps, core_ids=list(range(NCORE)))
    full = np.concatenate([res.results[c]["out"] for c in range(NCORE)], axis=0)
    return full.reshape(1, S, D)
